# version 01
# baseline (speedup 1.0000x reference)
import numpy as np
import ml_dtypes
from contextlib import ExitStack

import concourse.bass as bass
import concourse.mybir as mybir
from concourse.bass_utils import run_bass_kernel_spmd

F32 = mybir.dt.float32
BF16 = mybir.dt.bfloat16
I32 = mybir.dt.int32
AF = mybir.ActivationFunctionType
ALU = mybir.AluOpType
AX = mybir.AxisListType

D_MODEL = 1024
NFT = 8
T = 4096
LN_EPS = 1e-5



class Sem:
    def __init__(self, stk, nc, name):
        self.h = stk.enter_context(nc.semaphore(name))
        self.n = 0

    def tick(self, inst, k=1):
        inst.then_inc(self.h, k)
        self.n += k
        return inst


def _bf16(x):
    return np.asarray(x).astype(ml_dtypes.bfloat16)


def consts_np():
    c = {}
    c["id128"] = np.eye(128, dtype=np.float32)
    c["id128_16"] = np.eye(128, dtype=np.float32).astype(ml_dtypes.bfloat16)
    c["ones_row"] = np.ones((1, 128), dtype=np.float32)
    c["ones_col"] = np.ones((128, 1), dtype=np.float32)
    c["ones_col16"] = np.ones((128, 1), dtype=np.float32).astype(ml_dtypes.bfloat16)
    c["ut128"] = np.triu(np.ones((128, 128), dtype=np.float32))
    c["ut128s"] = np.triu(np.ones((128, 128), dtype=np.float32), 1)
    return c


class Consts:
    def __init__(self, nc, extra=()):
        cn = consts_np()
        self.np = cn
        self.dram = {k: nc.inline_tensor(v, name="c_" + k).ap()
                     for k, v in cn.items() if k in
                     ("id128", "id128_16", "ones_row", "ones_col", "ones_col16",
                      "ut128", "ut128s") or k in extra}


def load_consts(stk, nc, e, sem, names, cst):
    out = {}
    for n in names:
        v = cst.np[n]
        dt = BF16 if v.dtype == ml_dtypes.bfloat16 else F32
        t = stk.enter_context(nc.sbuf_tensor("sc_" + n, list(v.shape), dt))
        sem.tick(e.dma_start(t[:, :], cst.dram[n]), 16)
        out[n] = t
    return out


def emit_ln_fm(nc, B, pe, dv, ac, psB, psR, sb, x_ap, out_ap, W, cc,
               out_dtype_bf16_ap=None, gamma_col=None, beta_col=None,
               sq_ap=None, rows_ap=None, waits=()):
    def _sl(a, f):
        if isinstance(a, (list, tuple)):
            return a[f]
        return a[:, f * W:(f + 1) * W]

    one = sb["ones_col"]
    onesr = sb["ones_row"]

    @B.scalar
    def _(e):
        e.wait_ge(dv.h, dv.n)
        e.wait_ge(pe.h, pe.n)
        for s, n in waits:
            e.wait_ge(s.h, n if n is not None else s.n)
        last = None
        for f in range(NFT):
            last = e.activation(sq_ap[:, f * W:(f + 1) * W], _sl(x_ap, f), AF.Square)
        ac.tick(last)

    @B.tensor
    def _(e):
        e.wait_ge(ac.h, ac.n)
        for f in range(NFT):
            e.matmul(psR[0:1, 0:W], one[:, :], _sl(x_ap, f),
                     start=(f == 0), stop=(f == NFT - 1))
        for f in range(NFT):
            mm = e.matmul(psR[0:1, W:2 * W], one[:, :], sq_ap[:, f * W:(f + 1) * W],
                          start=(f == 0), stop=(f == NFT - 1))
        pe.tick(mm)

    @B.vector
    def _(e):
        e.wait_ge(pe.h, pe.n)
        e.tensor_scalar(rows_ap[0:1, 0:W], psR[0:1, 0:W], 1.0 / D_MODEL, None, ALU.mult)
        e.tensor_scalar(rows_ap[0:1, W:2 * W], psR[0:1, W:2 * W], 1.0 / D_MODEL, None, ALU.mult)
        e.tensor_tensor(rows_ap[0:1, 2 * W:3 * W], rows_ap[0:1, 0:W], rows_ap[0:1, 0:W], ALU.mult)
        e.tensor_tensor(rows_ap[0:1, W:2 * W], rows_ap[0:1, W:2 * W], rows_ap[0:1, 2 * W:3 * W], ALU.subtract)
        dv.tick(e.tensor_scalar(rows_ap[0:1, W:2 * W], rows_ap[0:1, W:2 * W], LN_EPS, None, ALU.add))

    @B.scalar
    def _(e):
        e.wait_ge(dv.h, dv.n)
        ac.tick(e.activation(rows_ap[0:1, 2 * W:3 * W], rows_ap[0:1, W:2 * W], AF.Sqrt))

    @B.vector
    def _(e):
        e.wait_ge(ac.h, ac.n)
        dv.tick(e.reciprocal(rows_ap[0:1, 3 * W:4 * W], rows_ap[0:1, 2 * W:3 * W]))

    @B.tensor
    def _(e):
        e.wait_ge(dv.h, dv.n)
        e.matmul(psB[:, 0:W], onesr[:, :], rows_ap[0:1, 0:W], start=True, stop=True)
        pe.tick(e.matmul(psB[:, W:2 * W], onesr[:, :], rows_ap[0:1, 3 * W:4 * W],
                         start=True, stop=True))

    @B.vector
    def _(e):
        e.wait_ge(pe.h, pe.n)
        last = None
        for f in range(NFT):
            o = _sl(out_ap, f)
            last = e.tensor_tensor(o, _sl(x_ap, f), psB[:, 0:W], ALU.subtract)
            last = e.tensor_tensor(o, o, psB[:, W:2 * W], ALU.mult)
            if gamma_col is not None:
                last = e.tensor_scalar(o, o, gamma_col[:, f:f + 1], beta_col[:, f:f + 1],
                                       ALU.mult, ALU.add)
            if out_dtype_bf16_ap is not None:
                last = e.tensor_copy(_sl(out_dtype_bf16_ap, f), o)
        dv.tick(last)


def stream_weightT(nc, B, pe, dv, lgs, ps_t, src_dram, dst_sb, n_po, n_fi,
                   id16, stage, cast=True, extra_waits=(), ac=None, stage32=None):
    slot_done_pe = [pe.n, pe.n]
    ac_slot = [ac.n, ac.n]
    for p in range(n_po):
        sl = p % 2
        off = sl * n_fi * 128
        lg = lgs[sl]

        @B.sync
        def _(e, p=p, sl=sl, off=off, lg=lg):
            e.wait_ge(ac.h, ac_slot[sl])
            lg.tick(e.dma_start(stage32[:, off:off + n_fi * 128],
                                src_dram[p * 128:(p + 1) * 128, :]), 16)

        @B.scalar
        def _(e, p=p, sl=sl, off=off, lg=lg):
            e.wait_ge(lg.h, lg.n)
            e.wait_ge(pe.h, slot_done_pe[sl])
            for s, n in extra_waits:
                e.wait_ge(s.h, n)
            ac.tick(e.copy(stage[:, off:off + n_fi * 128],
                           stage32[:, off:off + n_fi * 128]))
        ac_slot[sl] = ac.n

        for f in range(n_fi):
            @B.tensor
            def _(e, p=p, f=f, off=off):
                e.wait_ge(ac.h, ac.n)
                e.wait_ge(dv.h, dv.n)
                pe.tick(e.transpose(ps_t[:, :],
                                    stage[:, off + f * 128: off + (f + 1) * 128],
                                    id16[:, :]))

            @B.vector
            def _(e, p=p, f=f):
                e.wait_ge(pe.h, pe.n)
                dv.tick(e.tensor_copy(
                    dst_sb[:, f * n_po * 128 + p * 128:
                           f * n_po * 128 + (p + 1) * 128], ps_t[:, :]))
        slot_done_pe[sl] = pe.n



def build_phaseD(nc, x_in, wg_d, wu_d, wd_d, y_out, cst, final_tok_out=True):
    HID = 3072
    NHT = HID // 128
    TC = 256
    NCH = T // TC

    with ExitStack() as stk:
        e_ = stk.enter_context
        ld = Sem(stk, nc, "d_ld")
        lg = [Sem(stk, nc, "d_lg0"), Sem(stk, nc, "d_lg1")]
        pe = Sem(stk, nc, "d_pe")
        dv = Sem(stk, nc, "d_dv")
        ac = Sem(stk, nc, "d_ac")
        st = Sem(stk, nc, "d_st")

        s_wgt = e_(nc.sbuf_tensor("d_wgt", [128, NFT * NHT * 128], BF16))
        s_wut = e_(nc.sbuf_tensor("d_wut", [128, NFT * NHT * 128], BF16))
        s_wdt = e_(nc.sbuf_tensor("d_wdt", [128, NHT * NFT * 128], BF16))

        ps_mm = e_(nc.psum_tensor("d_psmm", [128, TC], F32))
        ps_mm2 = e_(nc.psum_tensor("d_psmm2", [128, TC], F32))
        ps_b = e_(nc.psum_tensor("d_psb", [128, 2 * TC], F32))
        ps_r = e_(nc.psum_tensor("d_psr", [1, 2 * TC], F32))
        ps_t = e_(nc.psum_tensor("d_pst", [128, 128], BF16))
        ps_tf = e_(nc.psum_tensor("d_pstf", [128, 128], F32))

        with nc.Block() as B:
            sb = load_consts(stk, nc, _mk_sync(B), ld,
                             ["ones_row", "ones_col", "id128_16", "id128"], cst)

            with nc.sbuf_tensor("d_stage", [128, 2 * NHT * 128], BF16) as s_stage, \
                 nc.sbuf_tensor("d_stage32", [128, 2 * NHT * 128], I32) as s_stage32:
                cw = [(ld, ld.n)]
                stream_weightT(nc, B, pe, dv, lg, ps_t, wg_d, s_wgt, NHT, NFT,
                               sb["id128_16"], s_stage, extra_waits=cw,
                               ac=ac, stage32=s_stage32)
                stream_weightT(nc, B, pe, dv, lg, ps_t, wu_d, s_wut, NHT, NFT,
                               sb["id128_16"], s_stage, extra_waits=cw,
                               ac=ac, stage32=s_stage32)
                stream_weightT(nc, B, pe, dv, lg, ps_t, wd_d, s_wdt, NFT, NHT,
                               sb["id128_16"], s_stage, extra_waits=cw,
                               ac=ac, stage32=s_stage32)

            s_x = e_(nc.sbuf_tensor("d_x", [128, NFT * TC], F32))
            s_xln = e_(nc.sbuf_tensor("d_xln", [128, NFT * TC], F32))
            s_xb = e_(nc.sbuf_tensor("d_xb", [128, NFT * TC], BF16))
            s_rows = e_(nc.sbuf_tensor("d_rows", [1, 4 * TC], F32))
            s_g = e_(nc.sbuf_tensor("d_g", [128, NHT * TC], BF16))
            s_dn = e_(nc.sbuf_tensor("d_dn", [128, NFT * TC], F32))
            s_out = e_(nc.sbuf_tensor("d_out", [128, NFT * TC], F32))
            s_otk = e_(nc.sbuf_tensor("d_otk", [128, (TC // 128) * D_MODEL], F32))

            for c in range(NCH):
                @B.sync
                def _(e, c=c):
                    e.wait_ge(st.h, st.n)
                    e.wait_ge(dv.h, dv.n)
                    ld.tick(e.dma_start(
                        s_x[:, :],
                        x_in.rearrange("p (f t) -> p f t", f=NFT)[:, :, c * TC:(c + 1) * TC]), 16)

                emit_ln_fm(nc, B, pe, dv, ac, ps_b, ps_r, sb, s_x, s_xln, TC, cst,
                           out_dtype_bf16_ap=s_xb, sq_ap=s_dn, rows_ap=s_rows,
                           waits=[(ld, None)])

                for m in range(NHT):
                    @B.tensor
                    def _(e, m=m):
                        e.wait_ge(dv.h, dv.n)
                        e.wait_ge(ac.h, ac.n)
                        for a in range(NFT):
                            e.matmul(ps_mm[:, :],
                                     s_wgt[:, a * NHT * 128 + m * 128: a * NHT * 128 + (m + 1) * 128],
                                     s_xb[:, a * TC:(a + 1) * TC],
                                     start=(a == 0), stop=(a == NFT - 1))
                        for a in range(NFT):
                            mm = e.matmul(ps_mm2[:, :],
                                          s_wut[:, a * NHT * 128 + m * 128: a * NHT * 128 + (m + 1) * 128],
                                          s_xb[:, a * TC:(a + 1) * TC],
                                          start=(a == 0), stop=(a == NFT - 1))
                        pe.tick(mm)

                    @B.scalar
                    def _(e, m=m):
                        e.wait_ge(pe.h, pe.n)
                        ac.tick(e.activation(s_g[:, m * TC:(m + 1) * TC], ps_mm[:, :],
                                             AF.Silu, scale=1.0 / 127.0))

                    @B.vector
                    def _(e, m=m):
                        e.wait_ge(ac.h, ac.n)
                        dv.tick(e.tensor_tensor(s_g[:, m * TC:(m + 1) * TC],
                                                s_g[:, m * TC:(m + 1) * TC],
                                                ps_mm2[:, :], ALU.mult))

                for m in range(NFT):
                    @B.tensor
                    def _(e, m=m):
                        e.wait_ge(dv.h, dv.n)
                        e.wait_ge(ac.h, ac.n)
                        for a in range(NHT):
                            mm = e.matmul(ps_mm[:, :],
                                          s_wdt[:, a * NFT * 128 + m * 128: a * NFT * 128 + (m + 1) * 128],
                                          s_g[:, a * TC:(a + 1) * TC],
                                          start=(a == 0), stop=(a == NHT - 1))
                        pe.tick(mm)

                    @B.scalar
                    def _(e, m=m):
                        e.wait_ge(pe.h, pe.n)
                        ac.tick(e.activation(s_dn[:, m * TC:(m + 1) * TC], ps_mm[:, :],
                                             AF.Copy, scale=1.0 / (127.0 * 127.0)))

                emit_ln_fm(nc, B, pe, dv, ac, ps_b, ps_r, sb, s_dn, s_xln, TC, cst,
                           sq_ap=s_otk, rows_ap=s_rows)

                @B.vector
                def _(e, c=c):
                    e.wait_ge(ac.h, ac.n)
                    e.wait_ge(dv.h, dv.n)
                    last = None
                    for f in range(NFT):
                        sl = slice(f * TC, (f + 1) * TC)
                        last = e.tensor_tensor(s_out[:, sl], s_x[:, sl], s_xln[:, sl], ALU.add)
                    dv.tick(last)

                if final_tok_out:
                    for tt in range(TC // 128):
                        for f in range(NFT):
                            @B.tensor
                            def _(e, tt=tt, f=f):
                                e.wait_ge(dv.h, dv.n)
                                e.wait_ge(st.h, st.n)
                                e.wait_ge(ac.h, ac.n)
                                pe.tick(e.transpose(
                                    ps_tf[:, :],
                                    s_out[:, f * TC + tt * 128: f * TC + (tt + 1) * 128],
                                    sb["id128"][:, :]))

                            @B.scalar
                            def _(e, tt=tt, f=f):
                                e.wait_ge(pe.h, pe.n)
                                ac.tick(e.copy(
                                    s_otk[:, tt * D_MODEL + f * 128: tt * D_MODEL + (f + 1) * 128],
                                    ps_tf[:, :]))

                    @B.sync
                    def _(e, c=c):
                        e.wait_ge(ac.h, ac.n)
                        st.tick(e.dma_start(
                            y_out[c * TC:(c + 1) * TC, :].rearrange("(n p) d -> p n d", p=128),
                            s_otk[:, :]), 16)
                else:
                    @B.sync
                    def _(e, c=c):
                        e.wait_ge(dv.h, dv.n)
                        st.tick(e.dma_start(
                            y_out.rearrange("p (f t) -> p f t", f=NFT)[:, :, c * TC:(c + 1) * TC],
                            s_out[:, :]), 16)

            @B.sync
            def _(e):
                e.wait_ge(st.h, st.n)


def _mk_sync(B):
    class _S:
        def dma_start(self, *a, **k):
            box = {}

            @B.sync
            def _(e):
                box["i"] = e.dma_start(*a, **k)
            return box["i"]
    return _S()



def build_phaseA(nc, x_in, wq_d, wk_d, wv_d, wo_d, bo_d, wi_d, bi_d, wf_d, bf_d,
                 gamma_d, beta_d, x_out, cst):
    NH, D = 16, 64
    CH = 256
    NCH = T // CH
    TB = 64
    NTB = CH // TB

    with ExitStack() as stk:
        e_ = stk.enter_context
        ld = Sem(stk, nc, "a_ld")
        lg = Sem(stk, nc, "a_lg")
        pe = Sem(stk, nc, "a_pe")
        dv = Sem(stk, nc, "a_dv")
        ac = Sem(stk, nc, "a_ac")
        st = Sem(stk, nc, "a_st")
        pk = Sem(stk, nc, "a_pk")
        pl = Sem(stk, nc, "a_pl")

        s_wq = e_(nc.sbuf_tensor("a_wq", [128, NFT * 1024], BF16))
        s_wk = e_(nc.sbuf_tensor("a_wk", [128, NFT * 1024], BF16))
        s_wv = e_(nc.sbuf_tensor("a_wv", [128, NFT * 1024], BF16))
        s_wo = e_(nc.sbuf_tensor("a_wo", [128, NFT * 1024], BF16))
        s_wif = e_(nc.sbuf_tensor("a_wif", [128, NFT * 32], F32))
        s_gb = e_(nc.sbuf_tensor("a_gb", [128, 2 * NFT], F32))
        s_boc = e_(nc.sbuf_tensor("a_boc", [128, NFT], F32))
        s_bifr = e_(nc.sbuf_tensor("a_bifr", [1, 32], F32))
        s_bifbc = e_(nc.sbuf_tensor("a_bifbc", [128, 32], F32))
        s_CN = e_(nc.sbuf_tensor("a_CN", [128, 8 * 65], F32))
        s_m = e_(nc.sbuf_tensor("a_m", [16, 1], F32))

        s_xt = e_(nc.sbuf_tensor("a_xt", [128, 2 * 1024], F32))
        s_xf = e_(nc.sbuf_tensor("a_xf", [128, NFT * CH], F32))
        s_xo = e_(nc.sbuf_tensor("a_xo", [128, NFT * CH], F32))
        s_xb = e_(nc.sbuf_tensor("a_xb", [128, NFT * CH], BF16))
        s_sq = e_(nc.sbuf_tensor("a_sq", [128, NFT * CH], F32))
        s_rows = e_(nc.sbuf_tensor("a_rows", [1, 4 * CH], F32))
        s_qf = e_(nc.sbuf_tensor("a_qf", [128, NFT * CH], BF16))
        s_kf = e_(nc.sbuf_tensor("a_kf", [128, NFT * CH], BF16))
        s_vf = e_(nc.sbuf_tensor("a_vf", [128, NFT * CH], BF16))
        s_of = e_(nc.sbuf_tensor("a_of", [128, NFT * CH], F32))
        s_kt = e_(nc.sbuf_tensor("a_kt", [128, 2 * 1024], BF16))
        s_vt = e_(nc.sbuf_tensor("a_vt", [128, 2 * 1024], BF16))
        s_ift = e_(nc.sbuf_tensor("a_ift", [128, 2 * 32], F32))
        s_at = e_(nc.sbuf_tensor("a_at", [128, 2 * 16], F32))
        s_afm = e_(nc.sbuf_tensor("a_afm", [16, CH], F32))
        s_arg = e_(nc.sbuf_tensor("a_arg", [16, CH], F32))
        s_wf4 = e_(nc.sbuf_tensor("a_wf4", [16, NTB * CH], F32))
        s_Ma = e_(nc.sbuf_tensor("a_Ma", [16, 128 + CH], F32))
        s_Mb = e_(nc.sbuf_tensor("a_Mb", [16, 128 + CH], F32))
        s_mu = e_(nc.sbuf_tensor("a_mu", [16, CH], F32))
        s_efm = e_(nc.sbuf_tensor("a_efm", [16, CH], BF16))
        s_nu = e_(nc.sbuf_tensor("a_nu", [16, NTB], F32))
        s_A = e_(nc.sbuf_tensor("a_A", [16, NTB], F32))
        s_nA = e_(nc.sbuf_tensor("a_nA", [16, NTB], F32))
        s_bm = e_(nc.sbuf_tensor("a_bm", [16, NTB], F32))
        s_blc = e_(nc.sbuf_tensor("a_blc", [16, 1], F32))
        s_r16 = e_(nc.sbuf_tensor("a_r16", [1, 64], F32))
        s_erow = e_(nc.sbuf_tensor("a_erow", [1, NH * CH], BF16))
        s_nurow = e_(nc.sbuf_tensor("a_nurow", [1, NH * NTB], F32))
        s_nuall = e_(nc.sbuf_tensor("a_nuall", [128, NH * NTB], F32))
        s_wt = e_(nc.sbuf_tensor("a_wt", [128, NTB * 32], F32))
        s_wt16 = e_(nc.sbuf_tensor("a_wt16", [128, NTB * 32], BF16))
        s_kw = e_(nc.sbuf_tensor("a_kw", [128, 6 * 64], BF16))
        s_vw = e_(nc.sbuf_tensor("a_vw", [128, 2 * 64], BF16))
        s_pn = e_(nc.sbuf_tensor("a_pn", [128, 6 * 64], BF16))
        s_pd = e_(nc.sbuf_tensor("a_pd", [128, 6 * 64], BF16))
        s_cnw = e_(nc.sbuf_tensor("a_cnw", [128, NTB * 65], BF16))
        s_cnt = e_(nc.sbuf_tensor("a_cnt", [128, 65], F32))
        s_rb = e_(nc.sbuf_tensor("a_rb", [128, 64], F32))
        s_dr = e_(nc.sbuf_tensor("a_dr", [1, 64], F32))
        s_hf = e_(nc.sbuf_tensor("a_hf", [128, NFT * CH], F32))
        s_ho = e_(nc.sbuf_tensor("a_ho", [128, NFT * CH], F32))

        bank1 = e_(nc.psum_tensor("a_bk1", [128, 512], F32))
        bank2 = e_(nc.psum_tensor("a_bk2", [128, 512], F32))
        bank3 = e_(nc.psum_tensor("a_bk3", [128, 512], F32))
        bank4 = e_(nc.psum_tensor("a_bk4", [128, 512], F32))
        bank5 = e_(nc.psum_tensor("a_bk5", [1, 512], F32))
        bank6 = e_(nc.psum_tensor("a_bk6", [128, 512], F32))
        ps_mm = bank1
        ps_b = bank1
        ps_sn0 = bank2[:, 0:256]
        ps_sn1 = bank2[:, 256:384]
        ps_tf = bank2[:, 384:512]
        ps_sd0 = bank3[:, 0:256]
        ps_sd1 = bank3[:, 256:384]
        ps_rb = bank3[:, 384:448]
        ps_b16 = bank3[:, 448:464]
        ps_p1 = bank4[:, 0:256]
        ps_den = bank5[0:1, 256:512]
        ps_car = bank3[0:1, 464:480]
        ps_r = bank5
        ps_cn = bank6[:, 0:65]

        M0 = np.triu(np.ones((64, 64), np.float32))
        M0 = np.concatenate([M0, np.zeros((64, 64), np.float32)], 0)
        M1 = np.concatenate([np.ones((64, 64), np.float32),
                             np.triu(np.ones((64, 64), np.float32))], 0)
        m0_d = nc.inline_tensor(M0, name="a_M0").ap()
        m1_d = nc.inline_tensor(M1, name="a_M1").ap()
        ninf = np.full((16, 128), -3.0e38, np.float32)
        ninf_d = nc.inline_tensor(ninf, name="a_ninf").ap()

        with nc.Block() as B:
            sb = load_consts(stk, nc, _mk_sync(B), ld,
                             ["ones_row", "ones_col", "ones_col16", "id128",
                              "id128_16", "ut128"], cst)
            s_M0 = e_(nc.sbuf_tensor("a_sM0", [128, 64], F32))
            s_M1 = e_(nc.sbuf_tensor("a_sM1", [128, 64], F32))

            @B.sync
            def _(e):
                ld.tick(e.dma_start(s_M0[:, :], m0_d), 16)
                ld.tick(e.dma_start(s_M1[:, :], m1_d), 16)
                ld.tick(e.dma_start(s_Ma[:, 0:128], ninf_d), 16)
                ld.tick(e.dma_start(s_Mb[:, 0:128], ninf_d), 16)
                ld.tick(e.dma_start(s_bifr[0:1, 0:16], bi_d[None, :]), 16)
                ld.tick(e.dma_start(s_bifr[0:1, 16:32], bf_d[None, :]), 16)

            @B.gpsimd
            def _(e):
                for w_d, s_w in ((wq_d, s_wq), (wk_d, s_wk), (wv_d, s_wv), (wo_d, s_wo)):
                    lg.tick(e.dma_start(
                        s_w[:, :].rearrange("p (a h) -> p a h", a=NFT),
                        w_d.rearrange("(a p) h -> p a h", p=128)), 16)
                lg.tick(e.dma_start(
                    s_wif.rearrange("p (a x) -> p a x", a=NFT)[:, :, 0:16],
                    wi_d.rearrange("(a p) h -> p a h", p=128)), 16)
                lg.tick(e.dma_start(
                    s_wif.rearrange("p (a x) -> p a x", a=NFT)[:, :, 16:32],
                    wf_d.rearrange("(a p) h -> p a h", p=128)), 16)
                lg.tick(e.dma_start(s_at[:, 0:2], x_in[0:128, 0:2]), 16)

            with nc.sbuf_tensor("a_grow", [1, 3 * 1024], F32) as s_grow:
                @B.sync
                def _(e):
                    ld.tick(e.dma_start(s_grow[0:1, 0:1024], gamma_d[None, :]), 16)
                    ld.tick(e.dma_start(s_grow[0:1, 1024:2048], beta_d[None, :]), 16)
                    ld.tick(e.dma_start(s_grow[0:1, 2048:3072], bo_d[None, :]), 16)

                for f in range(NFT):
                    @B.tensor
                    def _(e, f=f):
                        e.wait_ge(ld.h, ld.n)
                        e.wait_ge(dv.h, dv.n)
                        e.matmul(ps_tf[:, 0:1], s_grow[0:1, f * 128:(f + 1) * 128],
                                 sb["ones_row"][0:1, 0:1], start=True, stop=True)
                        e.matmul(ps_tf[:, 1:2], s_grow[0:1, 1024 + f * 128:1024 + (f + 1) * 128],
                                 sb["ones_row"][0:1, 0:1], start=True, stop=True)
                        pe.tick(e.matmul(ps_tf[:, 2:3], s_grow[0:1, 2048 + f * 128:2048 + (f + 1) * 128],
                                         sb["ones_row"][0:1, 0:1], start=True, stop=True))

                    @B.vector
                    def _(e, f=f):
                        e.wait_ge(pe.h, pe.n)
                        e.tensor_copy(s_gb[:, f:f + 1], ps_tf[:, 0:1])
                        e.tensor_copy(s_gb[:, NFT + f:NFT + f + 1], ps_tf[:, 1:2])
                        dv.tick(e.tensor_copy(s_boc[:, f:f + 1], ps_tf[:, 2:3]))

            @B.tensor
            def _(e):
                e.wait_ge(ld.h, ld.n)
                pe.tick(e.matmul(ps_mm[:, 0:32],
                                 sb["ones_row"][:, :], s_bifr[0:1, :], start=True, stop=True))

            @B.vector
            def _(e):
                e.wait_ge(pe.h, pe.n)
                e.tensor_copy(s_bifbc[:, :], ps_mm[:, 0:32])
                e.memset(s_CN[:, :], 0.0)
                e.memset(s_hf[:, :], 0.0)
                dv.tick(e.memset(s_m[:, :], 0.0))

            for c in range(A_NCH if "A_NCH" in globals() else NCH):
                @B.sync
                def _(e, c=c):
                    e.wait_ge(st.h, st.n)
                    e.wait_ge(dv.h, dv.n)
                    e.wait_ge(ac.h, ac.n)
                    e.wait_ge(pk.h, pk.n)
                    e.wait_ge(pl.h, pl.n)
                    ld.tick(e.dma_start(
                        s_xt[:, :],
                        x_in[c * CH:(c + 1) * CH, :].rearrange("(n p) d -> p n d", p=128)), 16)

                for n in range(2):
                    for f in range(NFT):
                        @B.tensor
                        def _(e, n=n, f=f):
                            e.wait_ge(ld.h, ld.n)
                            e.wait_ge(ac.h, ac.n)
                            pe.tick(e.transpose(ps_tf[:, :],
                                                s_xt[:, n * 1024 + f * 128:n * 1024 + (f + 1) * 128],
                                                sb["id128"][:, :]))

                        @B.scalar
                        def _(e, n=n, f=f):
                            e.wait_ge(pe.h, pe.n)
                            ac.tick(e.copy(s_xf[:, f * CH + n * 128:f * CH + (n + 1) * 128],
                                           ps_tf[:, :]))

                emit_ln_fm(nc, B, pe, dv, ac, ps_b, ps_r, sb, s_xf, s_xo, CH, cst,
                           out_dtype_bf16_ap=s_xb, sq_ap=s_sq, rows_ap=s_rows,
                           waits=[(ac, None)])

                for (s_w, s_dst, func, scale, bias) in (
                        (s_wq, s_qf, AF.Copy, 0.125, None),
                        (s_wk, s_kf, AF.Copy, 1.0, None),
                        (s_wv, s_vf, AF.Copy, 1.0, None),
                        (s_wo, s_of, AF.Sigmoid, 1.0, s_boc)):
                    for m in range(NFT):
                        @B.tensor
                        def _(e, s_w=s_w, m=m):
                            e.wait_ge(dv.h, dv.n)
                            e.wait_ge(ac.h, ac.n)
                            for a in range(NFT):
                                mm = e.matmul(ps_mm[:, 0:CH],
                                              s_w[:, a * 1024 + m * 128:a * 1024 + (m + 1) * 128],
                                              s_xb[:, a * CH:(a + 1) * CH],
                                              start=(a == 0), stop=(a == NFT - 1))
                            pe.tick(mm)

                        @B.scalar
                        def _(e, s_dst=s_dst, func=func, scale=scale, bias=bias, m=m):
                            e.wait_ge(pe.h, pe.n)
                            kw = {}
                            if bias is not None:
                                kw["bias"] = bias[:, m:m + 1]
                            ac.tick(e.activation(s_dst[:, m * CH:(m + 1) * CH], ps_mm[:, 0:CH],
                                                 func, scale=scale, **kw))

                for (s_w, s_dst) in ((s_wk, s_kt), (s_wv, s_vt)):
                    for tt in range(2):
                        for half in range(2):
                            @B.tensor
                            def _(e, s_w=s_w, tt=tt, half=half):
                                e.wait_ge(dv.h, dv.n)
                                e.wait_ge(ac.h, ac.n)
                                for a in range(NFT):
                                    mm = e.matmul(ps_mm[:, :],
                                                  s_xb[:, a * CH + tt * 128:a * CH + (tt + 1) * 128],
                                                  s_w[:, a * 1024 + half * 512:a * 1024 + (half + 1) * 512],
                                                  start=(a == 0), stop=(a == NFT - 1))
                                pe.tick(mm)

                            @B.scalar
                            def _(e, s_dst=s_dst, tt=tt, half=half):
                                e.wait_ge(pe.h, pe.n)
                                ac.tick(e.copy(s_dst[:, tt * 1024 + half * 512:tt * 1024 + (half + 1) * 512],
                                               ps_mm[:, :]))

                for tt in range(2):
                    @B.tensor
                    def _(e, tt=tt):
                        e.wait_ge(dv.h, dv.n)
                        e.wait_ge(ac.h, ac.n)
                        for a in range(NFT):
                            mm = e.matmul(ps_mm[:, 0:32],
                                          s_xo[:, a * CH + tt * 128:a * CH + (tt + 1) * 128],
                                          s_wif[:, a * 32:(a + 1) * 32],
                                          start=(a == 0), stop=(a == NFT - 1))
                        pe.tick(mm)

                    @B.vector
                    def _(e, tt=tt):
                        e.wait_ge(pe.h, pe.n)
                        dv.tick(e.tensor_tensor(s_ift[:, tt * 32:(tt + 1) * 32],
                                                ps_mm[:, 0:32], s_bifbc[:, :], ALU.add))

                @B.tensor
                def _(e, c=c):
                    e.wait_ge(dv.h, dv.n)
                    e.matmul(ps_b16[:, :], sb["ut128"][:, :], s_ift[:, 16:32],
                             start=True, stop=True)
                    e.matmul(ps_car[0:1, :], sb["ones_col"][:, :], s_ift[:, 16:32],
                             start=True, stop=True)
                    pe.tick(e.matmul(ps_r[0:1, 16:32], sb["ones_col"][:, :], s_ift[:, 16:32],
                                     start=True, stop=False))

                @B.vector
                def _(e, c=c):
                    e.wait_ge(pe.h, pe.n)
                    e.tensor_tensor(s_at[:, 0:16], s_ift[:, 0:16], ps_b16[:, :], ALU.subtract)
                    dv.tick(e.tensor_copy(s_r16[0:1, 0:16], ps_car[0:1, :]))

                @B.tensor
                def _(e, c=c):
                    e.wait_ge(dv.h, dv.n)
                    e.matmul(ps_b16[:, :], sb["ut128"][:, :], s_ift[:, 48:64],
                             start=True, stop=False)
                    e.matmul(ps_b16[:, :], sb["ones_row"][:, :], s_r16[0:1, 0:16],
                             start=False, stop=True)
                    pe.tick(e.matmul(ps_r[0:1, 16:32], sb["ones_col"][:, :], s_ift[:, 48:64],
                                     start=False, stop=True))

                @B.vector
                def _(e, c=c):
                    e.wait_ge(pe.h, pe.n)
                    e.tensor_tensor(s_at[:, 16:32], s_ift[:, 32:48], ps_b16[:, :], ALU.subtract)
                    dv.tick(e.tensor_copy(s_r16[0:1, 16:32], ps_r[0:1, 16:32]))

                for tt in range(2):
                    @B.tensor
                    def _(e, tt=tt):
                        e.wait_ge(dv.h, dv.n)
                        e.wait_ge(ac.h, ac.n)
                        pe.tick(e.transpose(ps_tf[0:16, 0:128], s_at[:, tt * 16:(tt + 1) * 16],
                                            sb["id128"][:, :]))

                    @B.scalar
                    def _(e, tt=tt):
                        e.wait_ge(pe.h, pe.n)
                        ac.tick(e.copy(s_afm[:, tt * 128:(tt + 1) * 128], ps_tf[0:16, 0:128]))

                @B.tensor
                def _(e, c=c):
                    e.wait_ge(dv.h, dv.n)
                    e.wait_ge(ac.h, ac.n)
                    pe.tick(e.matmul(ps_tf[0:16, 3:4], s_r16[0:1, 16:32], sb["ones_row"][0:1, 0:1],
                                     start=True, stop=True))

                @B.vector
                def _(e, c=c):
                    e.wait_ge(pe.h, pe.n)
                    e.wait_ge(ac.h, ac.n)
                    dv.tick(e.tensor_copy(s_blc[:, 0:1], ps_tf[0:16, 3:4]))

                @B.vector
                def _(e, c=c):
                    e.wait_ge(dv.h, dv.n)
                    e.tensor_copy(s_Ma[:, 128:128 + CH], s_afm[:, :])
                    srcb, dstb = s_Ma, s_Mb
                    k = 1
                    while k < CH:
                        e.tensor_tensor(dstb[:, 128:128 + CH], srcb[:, 128:128 + CH],
                                        srcb[:, 128 - k:128 + CH - k], ALU.max)
                        srcb, dstb = dstb, srcb
                        k *= 2
                    dv.tick(e.tensor_scalar(s_mu[:, :], srcb[:, 128:128 + CH], s_m[:, 0:1],
                                            None, ALU.max))

                @B.scalar
                def _(e, c=c):
                    e.wait_ge(dv.h, dv.n)
                    for tb in range(NTB):
                        e.copy(s_A[:, tb:tb + 1], s_mu[:, tb * TB + TB - 1:tb * TB + TB])
                    ac.tick(e.activation(s_nA[:, :], s_A[:, :], AF.Copy, scale=-1.0))

                @B.vector
                def _(e, c=c):
                    e.wait_ge(ac.h, ac.n)
                    for tb in range(NTB):
                        e.tensor_scalar(s_wf4[:, tb * CH:(tb + 1) * CH], s_afm[:, :],
                                        s_nA[:, tb:tb + 1], 0.0, ALU.add, ALU.min)
                    for tb in range(NTB):
                        e.tensor_scalar(s_arg[:, tb * TB:(tb + 1) * TB],
                                        s_mu[:, tb * TB:(tb + 1) * TB],
                                        s_nA[:, tb:tb + 1], None, ALU.add)
                    dv.tick(e.tensor_scalar(s_bm[:, :], s_nA[:, :], s_m[:, 0:1], None, ALU.add))

                @B.scalar
                def _(e, c=c):
                    e.wait_ge(dv.h, dv.n)
                    for tb in range(NTB):
                        e.activation(s_wf4[:, tb * CH:(tb + 1) * CH],
                                     s_wf4[:, tb * CH:(tb + 1) * CH], AF.Exp)
                    e.activation(s_efm[:, 0:CH], s_arg[:, 0:CH], AF.Exp)
                    ac.tick(e.activation(s_nu[:, :], s_bm[:, :], AF.Exp))

                @B.gpsimd
                def _(e, c=c):
                    e.wait_ge(ac.h, ac.n)
                    e.wait_ge(dv.h, dv.n)
                    pl.tick(e.tensor_tensor(s_m[:, 0:1], s_blc[:, 0:1], s_A[:, NTB - 1:NTB],
                                            ALU.add), 1)

                for tb in range(NTB):
                    for tt in range(2):
                        @B.tensor
                        def _(e, tb=tb, tt=tt):
                            e.wait_ge(ac.h, ac.n)
                            e.wait_ge(dv.h, dv.n)
                            pe.tick(e.transpose(
                                ps_tf[:, 0:16],
                                s_wf4[:, tb * CH + tt * 128:tb * CH + (tt + 1) * 128],
                                sb["id128"][0:16, 0:16]))

                        @B.vector
                        def _(e, tb=tb, tt=tt):
                            e.wait_ge(pe.h, pe.n)
                            e.tensor_copy(s_wt[:, tb * 32 + tt * 16:tb * 32 + (tt + 1) * 16],
                                          ps_tf[:, 0:16])
                            dv.tick(e.tensor_copy(
                                s_wt16[:, tb * 32 + tt * 16:tb * 32 + (tt + 1) * 16],
                                ps_tf[:, 0:16]))

                @B.gpsimd
                def _(e, c=c):
                    e.wait_ge(ac.h, ac.n)
                    e.wait_ge(dv.h, dv.n)
                    pk.tick(e.dma_start(s_erow[0:1, :], s_efm[:, :]), 16)
                    pk.tick(e.dma_start(s_nurow[0:1, :], s_nu[:, :]), 16)

                @B.tensor
                def _(e, c=c):
                    e.wait_ge(pk.h, pk.n)
                    pe.tick(e.matmul(ps_rb[:, :], sb["ones_row"][:, :], s_nurow[0:1, :],
                                     start=True, stop=True))

                @B.vector
                def _(e, c=c):
                    e.wait_ge(pe.h, pe.n)
                    dv.tick(e.tensor_copy(s_nuall[:, :], ps_rb[:, :]))

                COMBOS = [(0, 0), (0, 1), (0, 2), (0, 3), (1, 2), (1, 3)]
                for h in (A_HEADS if "A_HEADS" in globals() else range(NH)):
                    hb = 64 * (h % 2)
                    hcol = (h // 2) * CH
                    qh = s_qf[hb:hb + 64, hcol:hcol + CH]
                    kh = s_kf[hb:hb + 64, hcol:hcol + CH]
                    vh = s_vf[hb:hb + 64, hcol:hcol + CH]
                    cnblk = (h // 2) * 65

                    @B.vector
                    def _(e, h=h, hb=hb):
                        e.wait_ge(ac.h, ac.n)
                        e.wait_ge(pe.h, pe.n)
                        for ci, (tt, tb) in enumerate(COMBOS):
                            e.tensor_scalar(s_kw[:, ci * 64:(ci + 1) * 64],
                                            s_kt[:, tt * 1024 + 64 * h:tt * 1024 + 64 * (h + 1)],
                                            s_wt[:, tb * 32 + tt * 16 + h:tb * 32 + tt * 16 + h + 1],
                                            None, ALU.mult)
                        for tt in range(2):
                            e.tensor_scalar(s_vw[:, tt * 64:(tt + 1) * 64],
                                            s_vt[:, tt * 1024 + 64 * h:tt * 1024 + 64 * (h + 1)],
                                            s_wt[:, 3 * 32 + tt * 16 + h:3 * 32 + tt * 16 + h + 1],
                                            None, ALU.mult)
                        last = None
                        for tb in range(NTB):
                            last = e.tensor_scalar(
                                s_cnw[hb:hb + 64, tb * 65:(tb + 1) * 65],
                                s_CN[hb:hb + 64, cnblk:cnblk + 65],
                                s_nuall[hb:hb + 64, h * NTB + tb:h * NTB + tb + 1],
                                None, ALU.mult)
                        dv.tick(last)
                    dv_prep = dv.n

                    @B.tensor
                    def _(e, h=h):
                        e.wait_ge(dv.h, dv_prep)
                        e.matmul(ps_sn0[:, :], vh[:, 0:128], qh, start=True, stop=True)
                        e.matmul(ps_sn1[:, :], vh[:, 128:256], qh[:, 128:256], start=True, stop=True)
                        e.matmul(ps_sd0[:, :], kh[:, 0:128], qh, start=True, stop=True)
                        pe.tick(e.matmul(ps_sd1[:, :], kh[:, 128:256], qh[:, 128:256],
                                         start=True, stop=True))
                    pe_S = pe.n

                    @B.vector
                    def _(e, h=h):
                        e.wait_ge(pe.h, pe_S)
                        last = None
                        for (s_p, ps0, ps1) in ((s_pn, ps_sn0, ps_sn1), (s_pd, ps_sd0, ps_sd1)):
                            e.tensor_tensor(s_p[:, 0:64], ps0[:, 0:64], s_M0[:, :], ALU.mult)
                            e.tensor_tensor(s_p[:, 64:128], ps0[:, 64:128], s_M1[:, :], ALU.mult)
                            e.tensor_tensor(s_p[:, 256:320], ps1[:, 0:64], s_M0[:, :], ALU.mult)
                            last = e.tensor_tensor(s_p[:, 320:384], ps1[:, 64:128], s_M1[:, :], ALU.mult)
                        dv.tick(last)

                    @B.scalar
                    def _(e, h=h):
                        e.wait_ge(pe.h, pe_S)
                        e.copy(s_pn[:, 128:256], ps_sn0[:, 128:256])
                        ac.tick(e.copy(s_pd[:, 128:256], ps_sd0[:, 128:256]))
                    dv_pbar = dv.n
                    ac_pbar = ac.n

                    @B.tensor
                    def _(e, h=h, hb=hb):
                        e.wait_ge(dv.h, dv_pbar)
                        e.wait_ge(ac.h, ac_pbar)
                        for tb in range(NTB):
                            cols = slice(tb * TB, (tb + 1) * TB)
                            e.matmul(ps_p1[0:64, cols], s_cnw[hb:hb + 64, tb * 65:tb * 65 + 64],
                                     qh[:, cols], start=True, stop=False)
                            e.matmul(ps_den[0:1, cols], s_cnw[hb:hb + 64, tb * 65 + 64:(tb + 1) * 65],
                                     qh[:, cols], start=True, stop=False)
                            ci = COMBOS.index((0, tb))
                            e.matmul(ps_p1[0:64, cols], s_kw[:, ci * 64:(ci + 1) * 64],
                                     s_pn[:, tb * 64:(tb + 1) * 64],
                                     start=False, stop=(tb < 2))
                            e.matmul(ps_den[0:1, cols],
                                     s_wt16[:, tb * 32 + h:tb * 32 + h + 1],
                                     s_pd[:, tb * 64:(tb + 1) * 64],
                                     start=False, stop=(tb < 2))
                            if tb >= 2:
                                ci = COMBOS.index((1, tb))
                                e.matmul(ps_p1[0:64, cols], s_kw[:, ci * 64:(ci + 1) * 64],
                                         s_pn[:, 256 + (tb - 2) * 64:256 + (tb - 1) * 64],
                                         start=False, stop=True)
                                e.matmul(ps_den[0:1, cols],
                                         s_wt16[:, tb * 32 + 16 + h:tb * 32 + 16 + h + 1],
                                         s_pd[:, 256 + (tb - 2) * 64:256 + (tb - 1) * 64],
                                         start=False, stop=True)
                        for tt in range(2):
                            e.matmul(ps_cn[0:64, 0:64], s_vw[:, tt * 64:(tt + 1) * 64],
                                     s_kt[:, tt * 1024 + 64 * h:tt * 1024 + 64 * (h + 1)],
                                     start=(tt == 0), stop=(tt == 1))
                        for tt in range(2):
                            ci = COMBOS.index((tt, 3))
                            mm = e.matmul(ps_cn[0:64, 64:65], s_kw[:, ci * 64:(ci + 1) * 64],
                                          sb["ones_col16"][:, :], start=(tt == 0), stop=(tt == 1))
                        pe.tick(mm)
                    pe_p1 = pe.n

                    @B.vector
                    def _(e, h=h, hb=hb):
                        e.wait_ge(pe.h, pe_p1)
                        e.tensor_scalar(s_cnt[hb:hb + 64, :], s_CN[hb:hb + 64, cnblk:cnblk + 65],
                                        s_nuall[hb:hb + 64, h * NTB + 3:h * NTB + 4],
                                        None, ALU.mult)
                        dv.tick(e.tensor_tensor(s_CN[hb:hb + 64, cnblk:cnblk + 65],
                                                s_cnt[hb:hb + 64, :], ps_cn[0:64, :], ALU.add))

                    for tb in range(NTB):
                        cols = slice(tb * TB, (tb + 1) * TB)

                        @B.scalar
                        def _(e, h=h, tb=tb, cols=cols):
                            e.wait_ge(pe.h, pe.n)
                            ac.tick(e.activation(s_dr[0:1, :], ps_den[0:1, cols], AF.Abs))

                        @B.vector
                        def _(e, h=h, tb=tb):
                            e.wait_ge(ac.h, ac.n)
                            e.wait_ge(pk.h, pk.n)
                            e.tensor_tensor(s_r16[0:1, 0:64], s_dr[0:1, :],
                                            s_erow[0:1, h * CH + tb * TB:h * CH + (tb + 1) * TB],
                                            ALU.max)
                            dv.tick(e.reciprocal(s_dr[0:1, :], s_r16[0:1, 0:64]))

                        @B.tensor
                        def _(e, h=h, tb=tb):
                            e.wait_ge(dv.h, dv.n)
                            pe.tick(e.matmul(ps_rb[:, 0:64], sb["ones_row"][:, :], s_dr[0:1, :],
                                             start=True, stop=True))

                        @B.scalar
                        def _(e, h=h, tb=tb):
                            e.wait_ge(pe.h, pe.n)
                            ac.tick(e.copy(s_rb[:, :], ps_rb[:, 0:64]))

                        @B.vector
                        def _(e, h=h, hb=hb, tb=tb, cols=cols):
                            e.wait_ge(ac.h, ac.n)
                            hcols = slice((h // 2) * CH + tb * TB, (h // 2) * CH + (tb + 1) * TB)
                            e.tensor_tensor(s_rb[hb:hb + 64, :], ps_p1[0:64, cols],
                                            s_rb[hb:hb + 64, :], ALU.mult)
                            dv.tick(e.tensor_tensor(s_hf[hb:hb + 64, hcols], s_rb[hb:hb + 64, :],
                                                    s_of[hb:hb + 64, hcols], ALU.mult))

                emit_ln_fm(nc, B, pe, dv, ac, ps_b, ps_r, sb, s_hf, s_ho, CH, cst,
                           gamma_col=s_gb[:, 0:NFT], beta_col=s_gb[:, NFT:2 * NFT],
                           sq_ap=s_sq, rows_ap=s_rows)
                emit_ln_fm(nc, B, pe, dv, ac, ps_b, ps_r, sb, s_ho, s_xo, CH, cst,
                           sq_ap=s_sq, rows_ap=s_rows)

                @B.vector
                def _(e, c=c):
                    e.wait_ge(dv.h, dv.n)
                    last = None
                    for f in range(NFT):
                        sl2 = slice(f * CH, (f + 1) * CH)
                        last = e.tensor_tensor(s_ho[:, sl2], s_xf[:, sl2], s_xo[:, sl2], ALU.add)
                    dv.tick(last)

                @B.sync
                def _(e, c=c):
                    e.wait_ge(dv.h, dv.n)
                    st.tick(e.dma_start(
                        x_out.rearrange("p (f t) -> p f t", f=NFT)[:, :, c * CH:(c + 1) * CH],
                        s_ho[:, :]), 16)

            if "A_DBG" in globals():
                dbg_map = dict(hf=s_hf, erow=s_erow, wf4=s_wf4, mu=s_mu, afm=s_afm, efm=s_efm, arg=s_arg, nA=s_nA, Ma=s_Ma, Mb=s_Mb,
                               qf=s_qf, kf=s_kf, vf=s_vf, of=s_of, kt=s_kt, vt=s_vt,
                               CN=s_CN, m=s_m, nuall=s_nuall, wt=s_wt, at=s_at,
                               ift=s_ift, A=s_A, nu=s_nu, erow2=s_erow, xo=s_xo,
                               xb=s_xb, kw=s_kw, pn=s_pn, pd=s_pd, cnw=s_cnw,
                               ho=s_ho, dr=s_dr, rb=s_rb, blc=s_blc)
                @B.sync
                def _(e):
                    e.wait_ge(st.h, st.n)
                    e.wait_ge(dv.h, dv.n)
                    e.wait_ge(ac.h, ac.n)
                    e.wait_ge(pe.h, pe.n)
                    for nm in A_DBG:
                        t = dbg_map[nm]
                        o = nc.dram_tensor("dbg_" + nm, list(t.shape), t.dtype,
                                           kind="ExternalOutput").ap()
                        st.tick(e.dma_start(o, t[:, :] if len(t.shape) == 2 else t[:]), 16)
                    e.wait_ge(st.h, st.n)

            @B.sync
            def _(e):
                e.wait_ge(st.h, st.n)



def build_phaseC(nc, x_in, wc_d, bc_d, wo_d, bo_d, x_out, cst):
    NH, DH, DV = 16, 64, 512
    QC = 512
    NQC = T // QC
    NKT = T // 128

    outsp = nc.dram_tensor("c_outsp", [128, 64 * T], F32).ap()
    wc16 = nc.dram_tensor("c_wc16", [10240, 1024], BF16).ap()
    wo16 = nc.dram_tensor("c_wo16", [1024, 8192], BF16).ap()

    with ExitStack() as stk:
        e_ = stk.enter_context
        ld = Sem(stk, nc, "c_ld")
        lg = Sem(stk, nc, "c_lg")
        pe = Sem(stk, nc, "c_pe")
        dv = Sem(stk, nc, "c_dv")
        ac = Sem(stk, nc, "c_ac")
        st = Sem(stk, nc, "c_st")
        lw = [Sem(stk, nc, "c_lw0"), Sem(stk, nc, "c_lw1")]
        lo = [Sem(stk, nc, "c_lo0"), Sem(stk, nc, "c_lo1")]

        s_xln = e_(nc.sbuf_tensor("c_xln", [128, NFT * T], BF16))

        with nc.Block() as B:
            sb = load_consts(stk, nc, _mk_sync(B), ld,
                             ["ones_row", "ones_col", "id128", "id128_16"], cst)

            @B.gpsimd
            def _(e):
                for p in range(10):
                    lg.tick(e.dma_start(wc16[p * 1024:(p + 1) * 1024, :],
                                        wc_d[p * 1024:(p + 1) * 1024, :]), 16)
                for p in range(8):
                    lg.tick(e.dma_start(wo16[p * 128:(p + 1) * 128, :],
                                        wo_d[p * 128:(p + 1) * 128, :]), 16)
                lg.tick(e.dma_start(s_xln[0:128, 0:2], wc16[0:128, 0:2]), 16)

            with ExitStack() as pstk:
                p_ = pstk.enter_context
                s_x2c = p_(nc.sbuf_tensor("c_x2c", [128, NFT * QC], F32))
                s_sq = p_(nc.sbuf_tensor("c_sq", [128, NFT * QC], F32))
                s_xo = p_(nc.sbuf_tensor("c_xo", [128, NFT * QC], F32))
                s_rows = p_(nc.sbuf_tensor("c_rows", [1, 4 * QC], F32))
                ps_b = p_(nc.psum_tensor("c_psb", [128, 2 * QC], F32))
                ps_r = p_(nc.psum_tensor("c_psr", [1, 2 * QC], F32))
                for c in range(NQC):
                    @B.sync
                    def _(e, c=c):
                        e.wait_ge(dv.h, dv.n)
                        ld.tick(e.dma_start(
                            s_x2c[:, :],
                            x_in.rearrange("p (f t) -> p f t", f=NFT)[:, :, c * QC:(c + 1) * QC]), 16)

                    emit_ln_fm(nc, B, pe, dv, ac, ps_b, ps_r, sb, s_x2c, s_xo, QC, cst,
                               out_dtype_bf16_ap=[s_xln[:, f * T + c * QC: f * T + (c + 1) * QC]
                                                  for f in range(NFT)],
                               sq_ap=s_sq, rows_ap=s_rows, waits=[(ld, None)])

        with nc.Block() as B:
            with ExitStack() as hstk:
                h_ = hstk.enter_context
                s_wct = [h_(nc.sbuf_tensor("c_wct0", [128, NFT * 640], BF16)),
                         h_(nc.sbuf_tensor("c_wct1", [128, NFT * 640], BF16))]
                s_qkst = [h_(nc.sbuf_tensor("c_qkst0", [128, NFT * 128], BF16)),
                          h_(nc.sbuf_tensor("c_qkst1", [128, NFT * 128], BF16))]
                s_bcrow = [h_(nc.sbuf_tensor("c_bcrow0", [1, 640], F32)),
                           h_(nc.sbuf_tensor("c_bcrow1", [1, 640], F32))]
                s_q = h_(nc.sbuf_tensor("c_q", [64, T], BF16))
                s_k = h_(nc.sbuf_tensor("c_k", [64, T], BF16))
                s_vtok = h_(nc.sbuf_tensor("c_vtok", [128, NKT * DV], BF16))
                s_pT = h_(nc.sbuf_tensor("c_pT", [128, 2 * QC], BF16))
                s_dacc = h_(nc.sbuf_tensor("c_dacc", [128, QC], F32))
                s_ost = h_(nc.sbuf_tensor("c_ost", [128, 4 * QC], F32))
                s_bccol = h_(nc.sbuf_tensor("c_bccol", [128, 1], F32))
                s_rb = h_(nc.sbuf_tensor("c_rb", [128, QC], F32))
                s_sm = h_(nc.sbuf_tensor("c_sm", [128, 8], F32))
                s_smr = h_(nc.sbuf_tensor("c_smr", [1, 576], F32))
                ps_s = [h_(nc.psum_tensor("c_pss0", [128, QC], F32)),
                        h_(nc.psum_tensor("c_pss1", [128, QC], F32))]
                ps_o = [h_(nc.psum_tensor("c_pso%d" % i, [128, QC], F32)) for i in range(4)]
                ps_row = h_(nc.psum_tensor("c_psrow", [1, QC], F32))
                ps_t16 = h_(nc.psum_tensor("c_pst16", [128, 128], BF16))
                ps_c1 = ps_o[0]

                buf_free_pe = [0, 0]
                for h in range(NH):
                    bf = h % 2
                    lwb = lw[bf]

                    @B.sync
                    def _(e, h=h, bf=bf, lwb=lwb):
                        e.wait_ge(pe.h, buf_free_pe[bf])
                        e.wait_ge(lg.h, lg.n)
                        for a in range(NFT):
                            lwb.tick(e.dma_start(
                                s_wct[bf][:, a * 640 + 128:(a + 1) * 640],
                                wc16[2048 + 512 * h:2048 + 512 * (h + 1), a * 128:(a + 1) * 128],
                                transpose=True), 16)
                        lwb.tick(e.dma_start(s_qkst[bf][0:64, :],
                                             wc16[64 * h:64 * (h + 1), :]), 16)
                        lwb.tick(e.dma_start(s_qkst[bf][64:128, :],
                                             wc16[1024 + 64 * h:1024 + 64 * (h + 1), :]), 16)
                        lwb.tick(e.dma_start(s_bcrow[bf][0:1, 0:64], bc_d[None, 64 * h:64 * (h + 1)]), 16)
                        lwb.tick(e.dma_start(s_bcrow[bf][0:1, 64:128], bc_d[None, 1024 + 64 * h:1024 + 64 * (h + 1)]), 16)
                        lwb.tick(e.dma_start(s_bcrow[bf][0:1, 128:640],
                                             bc_d[None, 2048 + 512 * h:2048 + 512 * (h + 1)]), 16)

                    for a in range(NFT):
                        @B.tensor
                        def _(e, a=a, bf=bf, lwb=lwb):
                            e.wait_ge(lwb.h, lwb.n)
                            e.wait_ge(ac.h, ac.n)
                            pe.tick(e.transpose(ps_t16[:, :],
                                                s_qkst[bf][:, a * 128:(a + 1) * 128],
                                                sb["id128_16"][:, :]))

                        @B.scalar
                        def _(e, a=a, bf=bf):
                            e.wait_ge(pe.h, pe.n)
                            ac.tick(e.copy(s_wct[bf][:, a * 640:a * 640 + 128], ps_t16[:, :]))

                    @B.tensor
                    def _(e, bf=bf, lwb=lwb):
                        e.wait_ge(lwb.h, lwb.n)
                        e.wait_ge(dv.h, dv.n)
                        pe.tick(e.matmul(ps_c1[:, 0:1], s_bcrow[bf][0:1, 0:128],
                                         sb["ones_row"][0:1, 0:1], start=True, stop=True))

                    @B.vector
                    def _(e):
                        e.wait_ge(pe.h, pe.n)
                        dv.tick(e.tensor_copy(s_bccol[:, 0:1], ps_c1[:, 0:1]))

                    for tc in range(NQC):
                        @B.tensor
                        def _(e, tc=tc, bf=bf):
                            e.wait_ge(ac.h, ac.n)
                            e.wait_ge(dv.h, dv.n)
                            for a in range(NFT):
                                mm = e.matmul(ps_s[0][:, :],
                                              s_wct[bf][:, a * 640:a * 640 + 128],
                                              s_xln[:, a * T + tc * QC: a * T + (tc + 1) * QC],
                                              start=(a == 0), stop=(a == NFT - 1))
                            pe.tick(mm)

                        @B.scalar
                        def _(e, tc=tc):
                            e.wait_ge(pe.h, pe.n)
                            e.activation(s_q[:, tc * QC:(tc + 1) * QC], ps_s[0][0:64, :],
                                         AF.Lrelu, bias=s_bccol[0:64, 0:1], alpha=0.01)
                            ac.tick(e.activation(s_k[:, tc * QC:(tc + 1) * QC], ps_s[0][64:128, :],
                                                 AF.Lrelu, bias=s_bccol[64:128, 0:1], alpha=0.01))

                    for tt in range(NKT):
                        @B.tensor
                        def _(e, tt=tt, bf=bf):
                            e.wait_ge(ac.h, ac.n)
                            for a in range(NFT):
                                e.matmul(ps_s[1][:, :],
                                         s_xln[:, a * T + tt * 128: a * T + (tt + 1) * 128],
                                         s_wct[bf][:, a * 640 + 128:(a + 1) * 640],
                                         start=(a == 0), stop=False)
                            mm = e.matmul(ps_s[1][:, :], sb["ones_row"][:, :],
                                          s_bcrow[bf][0:1, 128:640], start=False, stop=True)
                            pe.tick(mm)

                        @B.scalar
                        def _(e, tt=tt):
                            e.wait_ge(pe.h, pe.n)
                            ac.tick(e.activation(s_vtok[:, tt * DV:(tt + 1) * DV], ps_s[1][:, :],
                                                 AF.Lrelu, alpha=0.01))

                    buf_free_pe[bf] = pe.n
                    ac_proj = ac.n

                    @B.vector
                    def _(e):
                        e.wait_ge(ac.h, ac.n)
                        e.tensor_reduce(s_sm[0:64, 0:1], s_q[:, :], AX.X, ALU.max,
                                        apply_absolute_value=True)
                        dv.tick(e.tensor_reduce(s_sm[64:128, 0:1], s_k[:, :], AX.X,
                                                ALU.max, apply_absolute_value=True))

                    @B.tensor
                    def _(e):
                        e.wait_ge(dv.h, dv.n)
                        pe.tick(e.transpose(ps_row[0:1, 0:128], s_sm[:, 0:1], sb["id128"][:, :]))

                    @B.vector
                    def _(e):
                        e.wait_ge(pe.h, pe.n)
                        e.tensor_reduce(s_smr[0:1, 0:1], ps_row[0:1, 0:64], AX.X, ALU.max)
                        e.tensor_reduce(s_smr[0:1, 1:2], ps_row[0:1, 64:128], AX.X, ALU.max)
                        e.tensor_tensor(s_smr[0:1, 0:1], s_smr[0:1, 0:1], s_smr[0:1, 1:2], ALU.mult)
                        dv.tick(e.tensor_scalar(s_smr[0:1, 0:1], s_smr[0:1, 0:1], -8.0, None, ALU.mult))

                    @B.tensor
                    def _(e):
                        e.wait_ge(dv.h, dv.n)
                        pe.tick(e.matmul(ps_c1[:, 0:1], sb["ones_row"][:, :],
                                         s_smr[0:1, 0:1], start=True, stop=True))

                    @B.vector
                    def _(e):
                        e.wait_ge(pe.h, pe.n)
                        dv.tick(e.tensor_copy(s_sm[:, 1:2], ps_c1[:, 0:1]))

                    dv_negch = dv.n

                    dv_oevac = dv.n
                    pe_den = pe.n
                    ac_base = ac_proj
                    pe_prev = pe.n
                    dv_prev = dv.n
                    for qc in range(NQC):
                        ac_exp = {}
                        pe_AV = {}
                        dv_dacc = {}
                        pe_S = {}
                        for kt in range(NKT):
                            sl = kt % 2

                            @B.tensor
                            def _(e, qc=qc, kt=kt, sl=sl):
                                if kt >= 2:
                                    e.wait_ge(ac.h, ac_exp[kt - 2])
                                else:
                                    e.wait_ge(ac.h, ac_base)
                                mm = e.matmul(ps_s[sl][:, :],
                                              s_k[:, kt * 128:(kt + 1) * 128],
                                              s_q[:, qc * QC:(qc + 1) * QC],
                                              start=True, stop=True)
                                pe.tick(mm)
                                pe_S[kt] = pe.n

                            @B.scalar
                            def _(e, qc=qc, kt=kt, sl=sl):
                                e.wait_ge(pe.h, pe_S[kt])
                                if kt >= 2:
                                    e.wait_ge(pe.h, pe_AV[kt - 2])
                                    e.wait_ge(dv.h, dv_dacc[kt - 2])
                                else:
                                    e.wait_ge(dv.h, max(dv_negch, dv_prev))
                                    e.wait_ge(pe.h, pe_prev)
                                inst = e.activation(s_pT[:, sl * QC:(sl + 1) * QC], ps_s[sl][:, :],
                                                    AF.Exp, bias=s_sm[:, 1:2], scale=0.125)
                                ac.tick(inst)
                                ac_exp[kt] = ac.n

                            @B.vector
                            def _(e, qc=qc, kt=kt, sl=sl):
                                e.wait_ge(ac.h, ac_exp[kt])
                                if kt == 0:
                                    e.wait_ge(pe.h, pe_den)
                                    inst = e.tensor_copy(s_dacc[:, :], s_pT[:, 0:QC])
                                else:
                                    inst = e.tensor_tensor(s_dacc[:, :], s_dacc[:, :],
                                                           s_pT[:, sl * QC:(sl + 1) * QC], ALU.add)
                                dv.tick(inst)
                                dv_dacc[kt] = dv.n

                            @B.tensor
                            def _(e, qc=qc, kt=kt, sl=sl):
                                e.wait_ge(ac.h, ac_exp[kt])
                                if kt == 0:
                                    e.wait_ge(dv.h, dv_oevac)
                                for dvi in range(4):
                                    mm = e.matmul(ps_o[dvi][:, :],
                                                  s_vtok[:, kt * DV + dvi * 128: kt * DV + (dvi + 1) * 128],
                                                  s_pT[:, sl * QC:(sl + 1) * QC],
                                                  start=(kt == 0), stop=(kt == NKT - 1))
                                pe.tick(mm)
                                pe_AV[kt] = pe.n

                        @B.tensor
                        def _(e, qc=qc):
                            e.wait_ge(dv.h, dv_dacc[NKT - 1])
                            pe.tick(e.matmul(ps_row[0:1, 0:QC], sb["ones_col"][:, :],
                                             s_dacc[:, :], start=True, stop=True))
                        pe_den = pe.n

                        @B.vector
                        def _(e, qc=qc):
                            e.wait_ge(pe.h, pe_den)
                            dv.tick(e.reciprocal(s_smr[0:1, 64:64 + QC], ps_row[0:1, 0:QC]))

                        @B.tensor
                        def _(e, qc=qc):
                            e.wait_ge(dv.h, dv.n)
                            pe.tick(e.matmul(ps_s[0][:, :], sb["ones_row"][:, :],
                                             s_smr[0:1, 64:64 + QC], start=True, stop=True))

                        @B.scalar
                        def _(e, qc=qc):
                            e.wait_ge(pe.h, pe.n)
                            ac.tick(e.copy(s_rb[:, :], ps_s[0][:, :]))
                        ac_base = ac.n

                        @B.vector
                        def _(e, qc=qc):
                            e.wait_ge(ac.h, ac.n)
                            e.wait_ge(st.h, st.n)
                            last = None
                            for dvi in range(4):
                                last = e.tensor_tensor(s_ost[:, dvi * QC:(dvi + 1) * QC],
                                                       ps_o[dvi][:, :], s_rb[:, :], ALU.mult)
                            dv.tick(last)
                        dv_oevac = dv.n

                        @B.sync
                        def _(e, qc=qc, h=h):
                            e.wait_ge(dv.h, dv_oevac)
                            st.tick(e.dma_start(
                                outsp.rearrange("p (v t) -> p v t", v=64)[:, 4 * h:4 * (h + 1),
                                                                          qc * QC:(qc + 1) * QC],
                                s_ost[:, :]), 16)
                        pe_prev = pe.n
                        dv_prev = dv.n

                    buf_free_pe[bf] = pe.n

        with nc.Block() as B:
            with ExitStack() as wstk:
                w_ = wstk.enter_context
                s_slab = [w_(nc.sbuf_tensor("c_slab0", [128, QC], BF16)),
                          w_(nc.sbuf_tensor("c_slab1", [128, QC], BF16))]
                s_ob = [w_(nc.sbuf_tensor("c_ob0", [128, QC], F32)),
                        w_(nc.sbuf_tensor("c_ob1", [128, QC], F32))]
                s_ob16 = [w_(nc.sbuf_tensor("c_ob16_0", [128, QC], BF16)),
                          w_(nc.sbuf_tensor("c_ob16_1", [128, QC], BF16))]
                s_x2c = w_(nc.sbuf_tensor("c_x2cw", [128, NFT * QC], F32))
                s_sq = w_(nc.sbuf_tensor("c_sqw", [128, NFT * QC], F32))
                s_proj = w_(nc.sbuf_tensor("c_proj", [128, NFT * QC], F32))
                s_xo = w_(nc.sbuf_tensor("c_xow", [128, NFT * QC], F32))
                s_rows = w_(nc.sbuf_tensor("c_rowsw", [1, 4 * QC], F32))
                s_bocol = w_(nc.sbuf_tensor("c_bocol", [128, 8], F32))
                s_borow = w_(nc.sbuf_tensor("c_borow", [1, 1024], F32))
                ps_p = [w_(nc.psum_tensor("c_psp%d" % i, [128, QC], F32)) for i in range(4)]
                ps_b = w_(nc.psum_tensor("c_psbw", [128, 2 * QC], F32))
                ps_r = w_(nc.psum_tensor("c_psrw", [1, 2 * QC], F32))
                ps_c2 = ps_p[0]

                @B.sync
                def _(e):
                    ld.tick(e.dma_start(s_borow[0:1, :], bo_d[None, :]), 16)

                @B.tensor
                def _(e):
                    e.wait_ge(ld.h, ld.n)
                    for f in range(NFT):
                        mm = e.matmul(ps_c2[:, f:f + 1], s_borow[0:1, f * 128:(f + 1) * 128],
                                      sb["ones_row"][0:1, 0:1], start=True, stop=True)
                    pe.tick(mm)

                @B.vector
                def _(e):
                    e.wait_ge(pe.h, pe.n)
                    dv.tick(e.tensor_copy(s_bocol[:, :], ps_c2[:, 0:8]))

                slab_free = {0: 0, 1: 0}
                for tc in range(NQC):
                    @B.sync
                    def _(e, tc=tc):
                        e.wait_ge(dv.h, dv.n)
                        ld.tick(e.dma_start(
                            s_x2c[:, :],
                            x_in.rearrange("p (f t) -> p f t", f=NFT)[:, :, tc * QC:(tc + 1) * QC]), 16)

                    for half in range(2):
                        for vt in range(64):
                            sl = vt % 2

                            @B.sync
                            def _(e, tc=tc, half=half, vt=vt, sl=sl):
                                e.wait_ge(pe.h, slab_free[sl])
                                lo[sl].tick(e.dma_start(
                                    s_slab[sl][:, :],
                                    wo16[half * 512:(half + 1) * 512, vt * 128:(vt + 1) * 128],
                                    transpose=True), 16)
                                lo[sl].tick(e.dma_start(
                                    s_ob[sl][:, :],
                                    outsp.rearrange("p (v t) -> p v t", v=64)[:, vt, tc * QC:(tc + 1) * QC]), 16)

                            @B.vector
                            def _(e, tc=tc, half=half, vt=vt, sl=sl):
                                e.wait_ge(lo[sl].h, lo[sl].n)
                                dv.tick(e.tensor_copy(s_ob16[sl][:, :], s_ob[sl][:, :]))

                            @B.tensor
                            def _(e, tc=tc, half=half, vt=vt, sl=sl):
                                e.wait_ge(dv.h, dv.n)
                                if vt == 0:
                                    e.wait_ge(ac.h, ac.n)
                                for dt in range(4):
                                    mm = e.matmul(ps_p[dt][:, :],
                                                  s_slab[sl][:, dt * 128:(dt + 1) * 128],
                                                  s_ob16[sl][:, :],
                                                  start=(vt == 0), stop=(vt == 63))
                                pe.tick(mm)
                                slab_free[sl] = pe.n

                        @B.scalar
                        def _(e, tc=tc, half=half):
                            e.wait_ge(pe.h, pe.n)
                            e.wait_ge(st.h, st.n)
                            last = None
                            for dt in range(4):
                                f = half * 4 + dt
                                last = e.activation(s_proj[:, f * QC:(f + 1) * QC], ps_p[dt][:, :],
                                                    AF.Lrelu, bias=s_bocol[:, f:f + 1], alpha=0.01)
                            ac.tick(last)

                    emit_ln_fm(nc, B, pe, dv, ac, ps_b, ps_r, sb, s_x2c, s_xo, QC, cst,
                               sq_ap=s_sq, rows_ap=s_rows, waits=[(ld, None)])

                    @B.vector
                    def _(e, tc=tc):
                        e.wait_ge(ac.h, ac.n)
                        e.wait_ge(dv.h, dv.n)
                        last = None
                        for f in range(NFT):
                            sl2 = slice(f * QC, (f + 1) * QC)
                            last = e.tensor_tensor(s_proj[:, sl2], s_proj[:, sl2], s_xo[:, sl2], ALU.add)
                        dv.tick(last)

                    emit_ln_fm(nc, B, pe, dv, ac, ps_b, ps_r, sb, s_proj, s_xo, QC, cst,
                               sq_ap=s_sq, rows_ap=s_rows)

                    @B.vector
                    def _(e, tc=tc):
                        e.wait_ge(dv.h, dv.n)
                        last = None
                        for f in range(NFT):
                            sl2 = slice(f * QC, (f + 1) * QC)
                            last = e.tensor_tensor(s_proj[:, sl2], s_x2c[:, sl2], s_xo[:, sl2], ALU.add)
                        dv.tick(last)

                    @B.sync
                    def _(e, tc=tc):
                        e.wait_ge(dv.h, dv.n)
                        e.wait_ge(st.h, st.n)
                        st.tick(e.dma_start(
                            x_out.rearrange("p (f t) -> p f t", f=NFT)[:, :, tc * QC:(tc + 1) * QC],
                            s_proj[:, :]), 16)

            @B.sync
            def _(e):
                e.wait_ge(st.h, st.n)




def build_phaseB(nc, x_in, wr_d, eb_d, w1_d, w2_d, sw1_d, sw2_d, cmask_d, x_out, cst):
    E, KTOP, CAP = 64, 6, 384
    TC = 512
    NCH = T // TC
    NTT = TC // 128
    BIG = 1.0e9

    xtk16 = nc.dram_tensor("b_xtk", [T, 1024], BF16).ap()
    w116 = nc.dram_tensor("b_w116", [E, 1024, 512], BF16).ap()
    w216 = nc.dram_tensor("b_w216", [E, 512, 1024], BF16).ap()
    tblf = nc.dram_tensor("b_tbl", [E * 512, 2], F32).ap()
    accs = [nc.dram_tensor("b_acc%d" % j, [T, 1024], F32).ap() for j in range(4)]
    shfm = nc.dram_tensor("b_shfm", [128, NFT * T], F32).ap()
    cc_in = nc.dram_tensor("b_ccin", [1, 64], F32).ap()
    cc_out = nc.dram_tensor("b_ccout", [8, 64], F32).ap()

    with ExitStack() as stk:
        e_ = stk.enter_context
        ld = Sem(stk, nc, "b_ld")
        lg = Sem(stk, nc, "b_lg")
        lw = [Sem(stk, nc, "b_lw0"), Sem(stk, nc, "b_lw1")]
        pe = Sem(stk, nc, "b_pe")
        dv = Sem(stk, nc, "b_dv")
        ac = Sem(stk, nc, "b_ac")
        st = Sem(stk, nc, "b_st")
        pl = Sem(stk, nc, "b_pl")
        cha = [Sem(stk, nc, "b_ch%d" % j) for j in range(4)]
        ccs = Sem(stk, nc, "b_ccs")

        s_tix = e_(nc.sbuf_tensor("b_tix", [128, NCH * 24], F32))
        s_tpr = e_(nc.sbuf_tensor("b_tpr", [128, NCH * 24], F32))
        s_trk = e_(nc.sbuf_tensor("b_trk", [128, NCH * 24], F32))
        s_iob = e_(nc.sbuf_tensor("b_iob", [128, 64], F32))
        s_base = e_(nc.sbuf_tensor("b_base", [1, 64], F32))
        s_gb = e_(nc.sbuf_tensor("b_gbase", [1, 64], F32))
        s_gbbc = e_(nc.sbuf_tensor("b_gbbc", [128, 64], F32))
        s_cm = e_(nc.sbuf_tensor("b_cm", [8, 1], F32))

        with nc.Block() as B:
            sb = load_consts(stk, nc, _mk_sync(B), ld,
                             ["ones_row", "ones_col", "id128", "id128_16",
                              "ut128s"], cst)
            with ExitStack() as b1:
                p_ = b1.enter_context
                s_sw1t = p_(nc.sbuf_tensor("b_sw1t", [128, NFT * 2048], BF16))
                s_sw2t = p_(nc.sbuf_tensor("b_sw2t", [128, 16 * 1024], BF16))
                s_wrt = p_(nc.sbuf_tensor("b_wrt", [128, NFT * 64], F32))
                s_wstage = p_(nc.sbuf_tensor("b_wstage", [128, 2 * 16 * 128], BF16))
                s_wstage32 = p_(nc.sbuf_tensor("b_wstage32", [128, 2 * 16 * 128], F32))
                s_rtn = p_(nc.sbuf_tensor("b_rtn", [64, 1024], F32))
                s_ebr = p_(nc.sbuf_tensor("b_ebr", [1, 64], F32))
                s_ebbc = p_(nc.sbuf_tensor("b_ebbc", [128, 64], F32))
                s_x = p_(nc.sbuf_tensor("b_x", [128, NFT * TC], F32))
                s_xo = p_(nc.sbuf_tensor("b_xo", [128, NFT * TC], F32))
                s_xb = p_(nc.sbuf_tensor("b_xb", [128, NFT * TC], BF16))
                s_rows = p_(nc.sbuf_tensor("b_rows", [1, 4 * TC], F32))
                s_xtk = p_(nc.sbuf_tensor("b_xtks", [128, NTT * 1024], BF16))
                s_g = p_(nc.sbuf_tensor("b_g", [128, 16 * TC], BF16))
                s_sh = p_(nc.sbuf_tensor("b_sh", [128, NFT * TC], F32))
                s_sig = p_(nc.sbuf_tensor("b_sig", [128, 256], F32))
                s_adj = p_(nc.sbuf_tensor("b_adj", [128, 256], F32))
                s_eq = p_(nc.sbuf_tensor("b_eq", [128, 256], F32))
                s_sel = p_(nc.sbuf_tensor("b_sel", [128, 256], F32))
                s_oh = p_(nc.sbuf_tensor("b_oh", [128, 256], F32))
                s_t2 = p_(nc.sbuf_tensor("b_t2", [128, 256], F32))
                s_av = p_(nc.sbuf_tensor("b_av", [128, 256], F32))
                s_mx = p_(nc.sbuf_tensor("b_mx", [128, 8], F32))
                s_ix = p_(nc.sbuf_tensor("b_ix", [128, 8], F32))
                s_raw = p_(nc.sbuf_tensor("b_raw", [128, 24], F32))
                s_rs = p_(nc.sbuf_tensor("b_rs", [128, 8], F32))
                s_cnt = p_(nc.sbuf_tensor("b_cnt", [128, 256], F32))
                s_bt = p_(nc.sbuf_tensor("b_bt", [1, 64], F32))

                bank_mm = p_(nc.psum_tensor("b_bmm", [128, 512], F32))
                bank_b = p_(nc.psum_tensor("b_bb", [128, 1024], F32))
                bank_r = p_(nc.psum_tensor("b_br", [1, 1024], F32))
                bank_t16 = p_(nc.psum_tensor("b_bt16", [128, 128], BF16))
                bank_tf = p_(nc.psum_tensor("b_btf", [128, 128], F32))
                bank_c = p_(nc.psum_tensor("b_bc", [128, 512], F32))
                ps_cnt = bank_c[:, 0:64]
                ps_rt = bank_c[:, 64:128]
                ps_tot = bank_c[0:1, 128:192]

                @B.sync
                def _(e):
                    ld.tick(e.dma_start(s_rtn[:, :], wr_d), 16)
                    ld.tick(e.dma_start(s_ebr[0:1, :], eb_d[None, :]), 16)
                    ld.tick(e.dma_start(s_cm[:, :], cmask_d), 16)

                @B.gpsimd
                def _(e):
                    e.iota(s_iob[:, :], [[1, 64]], channel_multiplier=0,
                           allow_small_or_imprecise_dtypes=True)
                    for ex8 in range(0, E, 8):
                        lg.tick(e.dma_start(w116[ex8:ex8 + 8], w1_d[ex8:ex8 + 8]), 16)
                        lg.tick(e.dma_start(w216[ex8:ex8 + 8], w2_d[ex8:ex8 + 8]), 16)
                    lg.tick(e.dma_start(s_mx[:, 6:8], w1_d[0, 0:128, 0:2]), 16)

                stream_weightT(nc, B, pe, dv, lw, bank_t16, sw1_d, s_sw1t, 16, NFT,
                               sb["id128_16"], s_wstage, extra_waits=[(ld, 3 * 16)],
                               ac=ac, stage32=s_wstage32)
                stream_weightT(nc, B, pe, dv, lw, bank_t16, sw2_d, s_sw2t, NFT, 16,
                               sb["id128_16"], s_wstage, extra_waits=[(ld, 3 * 16)],
                               ac=ac, stage32=s_wstage32)

                for a in range(NFT):
                    @B.tensor
                    def _(e, a=a):
                        e.wait_ge(ld.h, ld.n)
                        e.wait_ge(dv.h, dv.n)
                        pe.tick(e.transpose(bank_tf[:, 0:64],
                                            s_rtn[:, a * 128:(a + 1) * 128],
                                            sb["id128"][0:64, 0:64]))

                    @B.vector
                    def _(e, a=a):
                        e.wait_ge(pe.h, pe.n)
                        dv.tick(e.tensor_copy(s_wrt[:, a * 64:(a + 1) * 64],
                                              bank_tf[:, 0:64]))

                @B.tensor
                def _(e):
                    e.wait_ge(ld.h, ld.n)
                    pe.tick(e.matmul(bank_c[:, 384:448], sb["ones_row"][:, :],
                                     s_ebr[0:1, :], start=True, stop=True))

                @B.vector
                def _(e):
                    e.wait_ge(pe.h, pe.n)
                    e.tensor_copy(s_ebbc[:, :], bank_c[:, 384:448])
                    dv.tick(e.memset(s_base[0:1, :], 0.0))

                for c in range(NCH):
                    @B.sync
                    def _(e, c=c):
                        e.wait_ge(dv.h, dv.n)
                        e.wait_ge(st.h, st.n)
                        ld.tick(e.dma_start(
                            s_x[:, :],
                            x_in.rearrange("p (f t) -> p f t", f=NFT)[:, :, c * TC:(c + 1) * TC]), 16)

                    emit_ln_fm(nc, B, pe, dv, ac, bank_b, bank_r, sb, s_x, s_xo, TC, cst,
                               out_dtype_bf16_ap=s_xb, sq_ap=s_sh, rows_ap=s_rows,
                               waits=[(ld, None), (st, None)])

                    for tt in ([] if "xtk" in globals().get("B_SKIP_SET", set())
                               else range(NTT)):
                        for f in range(NFT):
                            @B.tensor
                            def _(e, tt=tt, f=f):
                                e.wait_ge(dv.h, dv.n)
                                e.wait_ge(ac.h, ac.n)
                                pe.tick(e.transpose(
                                    bank_t16[:, :],
                                    s_xb[:, f * TC + tt * 128:f * TC + (tt + 1) * 128],
                                    sb["id128_16"][:, :]))

                            @B.scalar
                            def _(e, tt=tt, f=f):
                                e.wait_ge(pe.h, pe.n)
                                ac.tick(e.copy(
                                    s_xtk[:, tt * 1024 + f * 128:tt * 1024 + (f + 1) * 128],
                                    bank_t16[:, :]))

                    if "xtk" not in globals().get("B_SKIP_SET", set()):
                        @B.sync
                        def _(e, c=c):
                            e.wait_ge(ac.h, ac.n)
                            ld.tick(e.dma_start(
                                xtk16[c * TC:(c + 1) * TC, :].rearrange("(n p) d -> p n d", p=128),
                                s_xtk[:, :]), 16)

                    for tt in ([] if "router" in globals().get("B_SKIP_SET", set())
                               else range(NTT)):
                        @B.tensor
                        def _(e, tt=tt):
                            e.wait_ge(dv.h, dv.n)
                            e.wait_ge(pl.h, pl.n)
                            e.wait_ge(ac.h, ac.n)
                            for a in range(NFT):
                                mm = e.matmul(ps_rt[:, :],
                                              s_xo[:, a * TC + tt * 128:a * TC + (tt + 1) * 128],
                                              s_wrt[:, a * 64:(a + 1) * 64],
                                              start=(a == 0), stop=(a == NFT - 1))
                            pe.tick(mm)

                        @B.scalar
                        def _(e, tt=tt):
                            e.wait_ge(pe.h, pe.n)
                            ac.tick(e.activation(s_sig[:, tt * 64:(tt + 1) * 64],
                                                 ps_rt[:, :], AF.Sigmoid))

                    for k in ([] if "topk" in globals().get("B_SKIP_SET", set())
                              else range(KTOP)):
                        @B.vector
                        def _(e, k=k):
                            e.wait_ge(pl.h, pl.n)
                            if k == 0:
                                e.wait_ge(ac.h, ac.n)
                                for tt in range(NTT):
                                    e.tensor_tensor(s_adj[:, tt * 64:(tt + 1) * 64],
                                                    s_sig[:, tt * 64:(tt + 1) * 64],
                                                    s_ebbc[:, :], ALU.add)
                            dv.tick(e.tensor_reduce(
                                s_mx[:, 0:4],
                                s_adj[:, :].rearrange("p (s e) -> p s e", s=NTT),
                                AX.X, ALU.max))

                        @B.gpsimd
                        def _(e, k=k):
                            e.wait_ge(dv.h, dv.n)
                            for tt in range(NTT):
                                e.tensor_scalar(s_eq[:, tt * 64:(tt + 1) * 64],
                                                s_adj[:, tt * 64:(tt + 1) * 64],
                                                s_mx[:, tt:tt + 1], None, ALU.is_ge)
                            e.tensor_scalar(s_sel[:, :], s_eq[:, :], -BIG, BIG,
                                            ALU.mult, ALU.add)
                            last = None
                            for tt in range(NTT):
                                last = e.tensor_tensor(s_sel[:, tt * 64:(tt + 1) * 64],
                                                       s_sel[:, tt * 64:(tt + 1) * 64],
                                                       s_iob[:, :], ALU.add)
                            pl.tick(last, 1)

                        @B.vector
                        def _(e, k=k):
                            e.wait_ge(pl.h, pl.n)
                            dv.tick(e.tensor_reduce(
                                s_ix[:, 0:4],
                                s_sel[:, :].rearrange("p (s e) -> p s e", s=NTT),
                                AX.X, ALU.min))

                        @B.gpsimd
                        def _(e, k=k, c=c):
                            e.wait_ge(dv.h, dv.n)
                            for tt in range(NTT):
                                e.tensor_scalar(s_oh[:, tt * 64:(tt + 1) * 64],
                                                s_iob[:, :], s_ix[:, tt:tt + 1], None,
                                                ALU.is_equal)
                            e.tensor_tensor(s_t2[:, :], s_oh[:, :], s_sig[:, :], ALU.mult)
                            if k == 0:
                                e.tensor_copy(s_av[:, :], s_oh[:, :])
                            else:
                                e.tensor_tensor(s_av[:, :], s_av[:, :], s_oh[:, :], ALU.add)
                            e.tensor_scalar(s_oh[:, :], s_oh[:, :], -2.0 * BIG, None, ALU.mult)
                            pl.tick(e.tensor_tensor(s_adj[:, :], s_adj[:, :], s_oh[:, :],
                                                    ALU.add), 1)

                        @B.vector
                        def _(e, k=k, c=c):
                            e.wait_ge(pl.h, pl.n)
                            e.tensor_reduce(s_raw[:, k * 4:(k + 1) * 4],
                                            s_t2[:, :].rearrange("p (s e) -> p s e", s=NTT),
                                            AX.X, ALU.add)
                            for tt in range(NTT):
                                dv.tick(e.tensor_copy(
                                    s_tix[:, c * 24 + tt * 6 + k:c * 24 + tt * 6 + k + 1],
                                    s_ix[:, tt:tt + 1]))

                    if "topk" not in globals().get("B_SKIP_SET", set()):
                        @B.vector
                        def _(e, c=c):
                            dv.tick(e.tensor_reduce(
                                s_rs[:, 0:4],
                                s_raw[:, :].rearrange("p (k s) -> p s k", s=4),
                                AX.X, ALU.add))

                        @B.gpsimd
                        def _(e, c=c):
                            e.wait_ge(dv.h, dv.n)
                            pl.tick(e.tensor_scalar(s_rs[:, 4:8], s_rs[:, 0:4], 1e-20, None,
                                                    ALU.add), 1)

                        @B.vector
                        def _(e, c=c):
                            e.wait_ge(pl.h, pl.n)
                            dv.tick(e.reciprocal(s_rs[:, 0:4], s_rs[:, 4:8]))

                        @B.gpsimd
                        def _(e, c=c):
                            e.wait_ge(dv.h, dv.n)
                            last = None
                            for k in range(KTOP):
                                for tt in range(NTT):
                                    last = e.tensor_scalar(
                                        s_tpr[:, c * 24 + tt * 6 + k:c * 24 + tt * 6 + k + 1],
                                        s_raw[:, k * 4 + tt:k * 4 + tt + 1],
                                        s_rs[:, tt:tt + 1], None, ALU.mult)
                            pl.tick(last, 1)

                        for tt in range(NTT):
                            @B.vector
                            def _(e, tt=tt, c=c):
                                e.wait_ge(pl.h, pl.n)
                                dv.tick(e.tensor_copy(s_bt[0:1, :], s_base[0:1, :]))

                            @B.tensor
                            def _(e, tt=tt, c=c):
                                e.wait_ge(dv.h, dv.n)
                                e.matmul(ps_cnt[:, :], sb["ut128s"][:, :],
                                         s_av[:, tt * 64:(tt + 1) * 64], start=True, stop=False)
                                e.matmul(ps_cnt[:, :], sb["ones_row"][:, :], s_bt[0:1, :],
                                         start=False, stop=True)
                                pe.tick(e.matmul(ps_tot[0:1, :], sb["ones_col"][:, :],
                                                 s_av[:, tt * 64:(tt + 1) * 64], start=True, stop=True))

                            @B.vector
                            def _(e, tt=tt, c=c):
                                e.wait_ge(pe.h, pe.n)
                                e.tensor_copy(s_cnt[:, tt * 64:(tt + 1) * 64], ps_cnt[:, :])
                                dv.tick(e.tensor_tensor(s_base[0:1, :], s_bt[0:1, :],
                                                        ps_tot[0:1, :], ALU.add))

                        for k in range(KTOP):
                            @B.gpsimd
                            def _(e, k=k, c=c):
                                e.wait_ge(dv.h, dv.n)
                                for tt in range(NTT):
                                    e.tensor_scalar(s_oh[:, tt * 64:(tt + 1) * 64],
                                                    s_iob[:, :],
                                                    s_tix[:, c * 24 + tt * 6 + k:c * 24 + tt * 6 + k + 1],
                                                    None, ALU.is_equal)
                                pl.tick(e.tensor_tensor(s_t2[:, :], s_oh[:, :], s_cnt[:, :],
                                                        ALU.mult), 1)

                            @B.vector
                            def _(e, k=k, c=c):
                                e.wait_ge(pl.h, pl.n)
                                dv.tick(e.tensor_reduce(
                                    s_mx[:, 4:8],
                                    s_t2[:, :].rearrange("p (s e) -> p s e", s=NTT),
                                    AX.X, ALU.add))

                            @B.gpsimd
                            def _(e, k=k, c=c):
                                e.wait_ge(dv.h, dv.n)
                                last = None
                                for tt in range(NTT):
                                    last = e.tensor_copy(
                                        s_trk[:, c * 24 + tt * 6 + k:c * 24 + tt * 6 + k + 1],
                                        s_mx[:, 4 + tt:5 + tt])
                                pl.tick(last, 1)

                    for m in range(16):
                        @B.tensor
                        def _(e, m=m):
                            e.wait_ge(ac.h, ac.n)
                            e.wait_ge(dv.h, dv.n)
                            for a in range(NFT):
                                mm = e.matmul(bank_mm[:, :],
                                              s_sw1t[:, a * 2048 + m * 128:a * 2048 + (m + 1) * 128],
                                              s_xb[:, a * TC:(a + 1) * TC],
                                              start=(a == 0), stop=(a == NFT - 1))
                            pe.tick(mm)

                        @B.scalar
                        def _(e, m=m):
                            e.wait_ge(pe.h, pe.n)
                            ac.tick(e.activation(s_g[:, m * TC:(m + 1) * TC], bank_mm[:, :],
                                                 AF.Gelu_apprx_tanh if "B_GCOPY" not in globals()
                                                 else AF.Copy))

                    for m in range(NFT):
                        @B.tensor
                        def _(e, m=m):
                            e.wait_ge(ac.h, ac.n)
                            for a in range(16):
                                mm = e.matmul(bank_mm[:, :],
                                              s_sw2t[:, a * 1024 + m * 128:a * 1024 + (m + 1) * 128],
                                              s_g[:, a * TC:(a + 1) * TC],
                                              start=(a == 0), stop=(a == 15))
                            pe.tick(mm)

                        @B.scalar
                        def _(e, m=m):
                            e.wait_ge(pe.h, pe.n)
                            e.wait_ge(st.h, st.n)
                            ac.tick(e.copy(s_sh[:, m * TC:(m + 1) * TC], bank_mm[:, :]))

                    @B.sync
                    def _(e, c=c):
                        e.wait_ge(ac.h, ac.n)
                        st.tick(e.dma_start(
                            shfm.rearrange("p (f t) -> p f t", f=NFT)[:, :, c * TC:(c + 1) * TC],
                            s_sh[:, :]), 16)

                if "B_DBG1" in globals():
                    dbg1 = dict(sw1t=s_sw1t, sw2t=s_sw2t, xb=s_xb, g=s_g, sh=s_sh,
                                xo=s_xo, wrt=s_wrt)
                    @B.sync
                    def _(e):
                        e.wait_ge(st.h, st.n)
                        e.wait_ge(dv.h, dv.n)
                        e.wait_ge(ac.h, ac.n)
                        e.wait_ge(pe.h, pe.n)
                        e.wait_ge(pl.h, pl.n)
                        for nm in B_DBG1:
                            t = dbg1[nm]
                            o = nc.dram_tensor("dbg1_" + nm, list(t.shape), t.dtype,
                                               kind="ExternalOutput").ap()
                            st.tick(e.dma_start(o, t[:, :]), 16)
                        e.wait_ge(st.h, st.n)

                B_SKIP = globals().get("B_SKIP_SET", set())
                @B.gpsimd
                def _(e):
                    e.wait_ge(dv.h, dv.n)
                    lg.tick(e.dma_start(cc_in, s_base[0:1, :]), 16)
                    e.wait_ge(lg.h, lg.n)
                    if "collective" not in B_SKIP:
                        e.collective_compute(
                            "AllGather", ALU.bypass,
                            replica_groups=[[0, 1, 2, 3, 4, 5, 6, 7]],
                            ins=[cc_in.opt()], outs=[cc_out.opt()],
                        ).then_inc(ccs.h, 1)
                        ccs.n += 1
                        e.wait_ge(ccs.h, ccs.n)
                    else:
                        lg.tick(e.dma_start(cc_out[0:1, :], s_base[0:1, :]), 16)
                    lg.tick(e.dma_start(s_eq[0:8, 0:64], cc_out), 16)

                @B.tensor
                def _(e):
                    e.wait_ge(lg.h, lg.n)
                    e.wait_ge(ld.h, ld.n)
                    pe.tick(e.matmul(bank_c[0:1, 192:256], s_cm[:, :], s_eq[0:8, 0:64],
                                     start=True, stop=True))

                @B.vector
                def _(e):
                    e.wait_ge(pe.h, pe.n)
                    dv.tick(e.tensor_copy(s_gb[0:1, :], bank_c[0:1, 192:256]))

                @B.tensor
                def _(e):
                    e.wait_ge(dv.h, dv.n)
                    pe.tick(e.matmul(bank_c[:, 256:320], sb["ones_row"][:, :],
                                     s_gb[0:1, :], start=True, stop=True))

                @B.vector
                def _(e):
                    e.wait_ge(pe.h, pe.n)
                    dv.tick(e.tensor_copy(s_gbbc[:, :], bank_c[:, 256:320]))

        with nc.Block() as B:
            with ExitStack() as b2:
                p_ = b2.enter_context
                s_sent = p_(nc.sbuf_tensor("b_sent", [128, 1024], F32))
                s_zero = p_(nc.sbuf_tensor("b_zero", [128, 2048], F32))
                s_oh2 = p_(nc.sbuf_tensor("b_oh2", [128, 64], F32))
                s_t3 = p_(nc.sbuf_tensor("b_t3", [128, 64], F32))
                s_bk = p_(nc.sbuf_tensor("b_bk", [128, 8], F32))
                s_tid = p_(nc.sbuf_tensor("b_tid", [128, 1], F32))
                s_pair = p_(nc.sbuf_tensor("b_pair", [128, 2], F32))
                s_offi = p_(nc.sbuf_tensor("b_offi", [128, 1], I32))
                s_trow = p_(nc.sbuf_tensor("b_trow", [1, 1024], F32))
                s_tidr = p_(nc.sbuf_tensor("b_tidr", [1, 384], F32))
                s_prr = p_(nc.sbuf_tensor("b_prr", [1, 384], F32))
                s_gidx = p_(nc.sbuf_tensor("b_gidx", [128, 3], I32))
                s_pcol = p_(nc.sbuf_tensor("b_pcol", [128, 3], F32))
                s_xst = p_(nc.sbuf_tensor("b_xst", [128, 3 * 1024], BF16))
                s_xsf = p_(nc.sbuf_tensor("b_xsf", [128, NFT * 384], BF16))
                s_w1 = p_(nc.sbuf_tensor("b_w1", [128, NFT * 512], BF16))
                s_w2 = p_(nc.sbuf_tensor("b_w2", [128, 4 * 1024], BF16))
                s_hm = p_(nc.sbuf_tensor("b_hm", [128, 4 * 384], BF16))
                s_ye = p_(nc.sbuf_tensor("b_ye", [128, NFT * 384], F32))
                s_yt = p_(nc.sbuf_tensor("b_yt", [128, 3 * 1024], F32))
                bank_mm2 = p_(nc.psum_tensor("b_bmm2", [128, 512], F32))
                bank_t16b = p_(nc.psum_tensor("b_bt16b", [128, 128], BF16))
                bank_tfb = p_(nc.psum_tensor("b_btfb", [128, 128], F32))

                @B.vector
                def _(e):
                    e.memset(s_sent[:, :], 0.0)
                    dv.tick(e.memset(s_zero[:, :], 0.0))

                @B.gpsimd
                def _(e):
                    e.wait_ge(dv.h, dv.n)
                    lg.tick(e.dma_start(tblf[:, :], s_sent[0:64, :]), 16)
                    for j in range(4):
                        for r in range(T * 1024 // (128 * 2048)):
                            lg.tick(e.dma_start(
                                accs[j][r * 256:(r + 1) * 256, :], s_zero[:, :]), 16)

                for ti in ([] if "scatter" in globals().get("B_SKIP_SET", set())
                           else range(T // 128)):
                    c, tt = ti // NTT, ti % NTT

                    @B.gpsimd
                    def _(e, ti=ti, c=c, tt=tt):
                        e.wait_ge(dv.h, dv.n)
                        e.wait_ge(lg.h, lg.n)
                        e.iota(s_tid[:, 0:1], [[1, 1]], channel_multiplier=1,
                               allow_small_or_imprecise_dtypes=True)
                        pl.tick(e.tensor_scalar(s_tid[:, 0:1], s_tid[:, 0:1],
                                                float(ti * 128), None, ALU.add), 1)

                    for k in range(KTOP):
                        col = c * 24 + tt * 6 + k

                        @B.gpsimd
                        def _(e, col=col, k=k):
                            e.tensor_scalar(s_oh2[:, :], s_iob[:, :],
                                            s_tix[:, col:col + 1], None, ALU.is_equal)
                            pl.tick(e.tensor_tensor(s_t3[:, :], s_oh2[:, :], s_gbbc[:, :],
                                                    ALU.mult), 1)

                        @B.vector
                        def _(e, col=col, k=k):
                            e.wait_ge(pl.h, pl.n)
                            dv.tick(e.tensor_reduce(s_bk[:, 0:1], s_t3[:, :], AX.X, ALU.add))

                        @B.gpsimd
                        def _(e, col=col, k=k):
                            e.wait_ge(dv.h, dv.n)
                            e.tensor_tensor(s_bk[:, 1:2], s_bk[:, 0:1],
                                            s_trk[:, col:col + 1], ALU.add)
                            e.tensor_scalar(s_bk[:, 2:3], s_bk[:, 1:2], float(CAP) - 0.5,
                                            None, ALU.is_lt)
                            e.tensor_tensor(s_pair[:, 1:2], s_bk[:, 2:3],
                                            s_tpr[:, col:col + 1], ALU.mult)
                            e.tensor_copy(s_pair[:, 0:1], s_tid[:, 0:1])
                            e.tensor_scalar(s_bk[:, 3:4], s_trk[:, col:col + 1], 511.0,
                                            None, ALU.min)
                            e.tensor_scalar(s_bk[:, 5:6], s_tix[:, col:col + 1], 512.0,
                                            s_bk[:, 3:4], ALU.mult, ALU.add)
                            pl.tick(e.tensor_copy(s_offi[:, 0:1], s_bk[:, 5:6]), 1)

                        @B.gpsimd
                        def _(e, col=col, k=k):
                            e.wait_ge(pl.h, pl.n)
                            lg.tick(e.indirect_dma_start(
                                tblf, bass.IndirectOffsetOnAxis(s_offi[:, 0:1], 0),
                                s_pair[:, :], None), 16)

                chain_n = [0, 0, 0, 0]
                for ex in ([] if "experts" in globals().get("B_SKIP_SET", set())
                           else range(E)):
                    j = ex % 4

                    @B.sync
                    def _(e, ex=ex):
                        e.wait_ge(lg.h, lg.n)
                        e.wait_ge(dv.h, dv.n)
                        ld.tick(e.dma_start(
                            s_trow[0:1, :],
                            tblf[ex * 512:(ex + 1) * 512, :]), 16)

                    @B.vector
                    def _(e, ex=ex):
                        e.wait_ge(ld.h, ld.n)
                        e.tensor_copy(s_tidr[0:1, :],
                                      s_trow[0:1, :].rearrange("o (s two) -> o s two", two=2)[:, 0:384, 0:1])
                        dv.tick(e.tensor_copy(
                            s_prr[0:1, :],
                            s_trow[0:1, :].rearrange("o (s two) -> o s two", two=2)[:, 0:384, 1:2]))

                    for a in range(3):
                        @B.tensor
                        def _(e, a=a, ex=ex):
                            e.wait_ge(dv.h, dv.n)
                            e.matmul(bank_tfb[:, 0:1],
                                     s_tidr[0:1, a * 128:(a + 1) * 128],
                                     sb["ones_row"][0:1, 0:1], start=True, stop=True)
                            pe.tick(e.matmul(bank_tfb[:, 1:2],
                                             s_prr[0:1, a * 128:(a + 1) * 128],
                                             sb["ones_row"][0:1, 0:1], start=True, stop=True))

                        @B.vector
                        def _(e, a=a, ex=ex):
                            e.wait_ge(pe.h, pe.n)
                            e.tensor_copy(s_gidx[:, a:a + 1], bank_tfb[:, 0:1])
                            dv.tick(e.tensor_copy(s_pcol[:, a:a + 1], bank_tfb[:, 1:2]))

                    @B.gpsimd
                    def _(e, ex=ex):
                        e.wait_ge(dv.h, dv.n)
                        e.wait_ge(ac.h, ac.n)
                        for a in range(3):
                            lg.tick(e.indirect_dma_start(
                                s_xst[:, a * 1024:(a + 1) * 1024], None,
                                xtk16, bass.IndirectOffsetOnAxis(s_gidx[:, a:a + 1], 0)), 16)
                        lg.tick(e.dma_start(
                            s_w1[:, :].rearrange("p (a h) -> p a h", a=NFT),
                            w116[ex].rearrange("(a p) h -> p a h", p=128)), 16)
                        lg.tick(e.dma_start(
                            s_w2[:, :].rearrange("p (a h) -> p a h", a=4),
                            w216[ex].rearrange("(a p) h -> p a h", p=128)), 16)

                    for a in range(3):
                        for f in range(NFT):
                            @B.tensor
                            def _(e, a=a, f=f, ex=ex):
                                e.wait_ge(lg.h, lg.n)
                                e.wait_ge(ac.h, ac.n)
                                pe.tick(e.transpose(
                                    bank_t16b[:, :],
                                    s_xst[:, a * 1024 + f * 128:a * 1024 + (f + 1) * 128],
                                    sb["id128_16"][:, :]))

                            @B.scalar
                            def _(e, a=a, f=f, ex=ex):
                                e.wait_ge(pe.h, pe.n)
                                ac.tick(e.copy(
                                    s_xsf[:, f * 384 + a * 128:f * 384 + (a + 1) * 128],
                                    bank_t16b[:, :]))

                    for m in range(4):
                        @B.tensor
                        def _(e, m=m, ex=ex):
                            e.wait_ge(ac.h, ac.n)
                            for a in range(NFT):
                                mm = e.matmul(bank_mm2[:, 0:384],
                                              s_w1[:, a * 512 + m * 128:a * 512 + (m + 1) * 128],
                                              s_xsf[:, a * 384:(a + 1) * 384],
                                              start=(a == 0), stop=(a == NFT - 1))
                            pe.tick(mm)

                        @B.scalar
                        def _(e, m=m, ex=ex):
                            e.wait_ge(pe.h, pe.n)
                            ac.tick(e.activation(s_hm[:, m * 384:(m + 1) * 384],
                                                 bank_mm2[:, 0:384], AF.Gelu_apprx_tanh))

                    for m in range(NFT):
                        @B.tensor
                        def _(e, m=m, ex=ex):
                            e.wait_ge(ac.h, ac.n)
                            e.wait_ge(dv.h, dv.n)
                            for a in range(4):
                                mm = e.matmul(bank_mm2[:, 0:384],
                                              s_w2[:, a * 1024 + m * 128:a * 1024 + (m + 1) * 128],
                                              s_hm[:, a * 384:(a + 1) * 384],
                                              start=(a == 0), stop=(a == 3))
                            pe.tick(mm)

                        @B.scalar
                        def _(e, m=m, ex=ex):
                            e.wait_ge(pe.h, pe.n)
                            ac.tick(e.copy(s_ye[:, m * 384:(m + 1) * 384], bank_mm2[:, 0:384]))

                    for a in range(3):
                        for f in range(NFT):
                            @B.tensor
                            def _(e, a=a, f=f, ex=ex):
                                e.wait_ge(ac.h, ac.n)
                                e.wait_ge(dv.h, dv.n)
                                pe.tick(e.transpose(
                                    bank_tfb[:, :],
                                    s_ye[:, f * 384 + a * 128:f * 384 + (a + 1) * 128],
                                    sb["id128"][:, :]))

                            @B.vector
                            def _(e, a=a, f=f, ex=ex):
                                e.wait_ge(pe.h, pe.n)
                                dv.tick(e.tensor_copy(
                                    s_yt[:, a * 1024 + f * 128:a * 1024 + (f + 1) * 128],
                                    bank_tfb[:, :]))

                    @B.vector
                    def _(e, ex=ex):
                        last = None
                        for a in range(3):
                            last = e.tensor_scalar(s_yt[:, a * 1024:(a + 1) * 1024],
                                                   s_yt[:, a * 1024:(a + 1) * 1024],
                                                   s_pcol[:, a:a + 1], None, ALU.mult)
                        dv.tick(last)

                    if "scatteradd" not in globals().get("B_SKIP_SET", set()):
                        @B.gpsimd
                        def _(e, ex=ex, j=j):
                            e.wait_ge(dv.h, dv.n)
                            e.wait_ge(cha[j].h, chain_n[j])
                            for a in range(3):
                                cha[j].tick(e.indirect_dma_start(
                                    accs[j], bass.IndirectOffsetOnAxis(s_gidx[:, a:a + 1], 0),
                                    s_yt[:, a * 1024:(a + 1) * 1024], None,
                                    compute_op=ALU.add), 16)
                            chain_n[j] = cha[j].n

        with nc.Block() as B:
            with ExitStack() as b3:
                p_ = b3.enter_context
                s_x = p_(nc.sbuf_tensor("b_x3", [128, NFT * TC], F32))
                s_sh = p_(nc.sbuf_tensor("b_sh3", [128, NFT * TC], F32))
                s_a0 = p_(nc.sbuf_tensor("b_a0", [128, NTT * 1024], F32))
                s_a1 = p_(nc.sbuf_tensor("b_a1", [128, NTT * 1024], F32))
                s_mo = p_(nc.sbuf_tensor("b_mo", [128, NFT * TC], F32))
                s_xo = p_(nc.sbuf_tensor("b_xo3", [128, NFT * TC], F32))
                s_sq = p_(nc.sbuf_tensor("b_sq3", [128, NFT * TC], F32))
                s_rows = p_(nc.sbuf_tensor("b_rows3", [1, 4 * TC], F32))
                bank_b3 = p_(nc.psum_tensor("b_bb3", [128, 1024], F32))
                bank_r3 = p_(nc.psum_tensor("b_br3", [1, 1024], F32))
                bank_tf3 = p_(nc.psum_tensor("b_btf3", [128, 128], F32))
                sb3 = sb

                for c in range(NCH):
                    @B.sync
                    def _(e, c=c):
                        e.wait_ge(dv.h, dv.n)
                        e.wait_ge(st.h, st.n)
                        for j in range(4):
                            e.wait_ge(cha[j].h, cha[j].n)
                        ld.tick(e.dma_start(
                            s_x[:, :],
                            x_in.rearrange("p (f t) -> p f t", f=NFT)[:, :, c * TC:(c + 1) * TC]), 16)
                        ld.tick(e.dma_start(
                            s_sh[:, :],
                            shfm.rearrange("p (f t) -> p f t", f=NFT)[:, :, c * TC:(c + 1) * TC]), 16)
                        ld.tick(e.dma_start(
                            s_a0[:, :],
                            accs[0][c * TC:(c + 1) * TC, :].rearrange("(n p) d -> p n d", p=128)), 16)
                        ld.tick(e.dma_start(
                            s_a1[:, :],
                            accs[1][c * TC:(c + 1) * TC, :].rearrange("(n p) d -> p n d", p=128)), 16)

                    @B.vector
                    def _(e, c=c):
                        e.wait_ge(ld.h, ld.n)
                        dv.tick(e.tensor_tensor(s_a0[:, :], s_a0[:, :], s_a1[:, :], ALU.add))

                    @B.sync
                    def _(e, c=c):
                        e.wait_ge(dv.h, dv.n)
                        ld.tick(e.dma_start(
                            s_a1[:, :],
                            accs[2][c * TC:(c + 1) * TC, :].rearrange("(n p) d -> p n d", p=128)), 16)

                    @B.vector
                    def _(e, c=c):
                        e.wait_ge(ld.h, ld.n)
                        dv.tick(e.tensor_tensor(s_a0[:, :], s_a0[:, :], s_a1[:, :], ALU.add))

                    @B.sync
                    def _(e, c=c):
                        e.wait_ge(dv.h, dv.n)
                        ld.tick(e.dma_start(
                            s_a1[:, :],
                            accs[3][c * TC:(c + 1) * TC, :].rearrange("(n p) d -> p n d", p=128)), 16)

                    @B.vector
                    def _(e, c=c):
                        e.wait_ge(ld.h, ld.n)
                        dv.tick(e.tensor_tensor(s_a0[:, :], s_a0[:, :], s_a1[:, :], ALU.add))

                    for tt in range(NTT):
                        for f in range(NFT):
                            @B.tensor
                            def _(e, tt=tt, f=f):
                                e.wait_ge(dv.h, dv.n)
                                e.wait_ge(ac.h, ac.n)
                                pe.tick(e.transpose(
                                    bank_tf3[:, :],
                                    s_a0[:, tt * 1024 + f * 128:tt * 1024 + (f + 1) * 128],
                                    sb3["id128"][:, :]))

                            @B.scalar
                            def _(e, tt=tt, f=f):
                                e.wait_ge(pe.h, pe.n)
                                ac.tick(e.copy(
                                    s_mo[:, f * TC + tt * 128:f * TC + (tt + 1) * 128],
                                    bank_tf3[:, :]))

                    @B.vector
                    def _(e, c=c):
                        e.wait_ge(ac.h, ac.n)
                        last = None
                        for f in range(NFT):
                            sl2 = slice(f * TC, (f + 1) * TC)
                            last = e.tensor_tensor(s_mo[:, sl2], s_mo[:, sl2], s_sh[:, sl2],
                                                   ALU.add)
                        dv.tick(last)

                    emit_ln_fm(nc, B, pe, dv, ac, bank_b3, bank_r3, sb3, s_mo, s_xo, TC, cst,
                               sq_ap=s_sq, rows_ap=s_rows)

                    @B.vector
                    def _(e, c=c):
                        e.wait_ge(dv.h, dv.n)
                        last = None
                        for f in range(NFT):
                            sl2 = slice(f * TC, (f + 1) * TC)
                            last = e.tensor_tensor(s_mo[:, sl2], s_x[:, sl2], s_xo[:, sl2],
                                                   ALU.add)
                        dv.tick(last)

                    @B.sync
                    def _(e, c=c):
                        e.wait_ge(dv.h, dv.n)
                        st.tick(e.dma_start(
                            x_out.rearrange("p (f t) -> p f t", f=NFT)[:, :, c * TC:(c + 1) * TC],
                            s_mo[:, :]), 16)

                @B.sync
                def _(e):
                    e.wait_ge(st.h, st.n)

        if "B_DBG" in globals():
            dbg_map = dict(tix=s_tix, tpr=s_tpr, trk=s_trk, base=s_base, gb=s_gb,
                           gbbc=s_gbbc, iob=s_iob, shfm=shfm, acc0=accs[0],
                           acc1=accs[1], acc2=accs[2], acc3=accs[3], xtk=xtk16,
                           tbl=tblf)
            with nc.Block() as B:
                @B.sync
                def _(e):
                    e.wait_ge(st.h, st.n)
                    e.wait_ge(dv.h, dv.n)
                    e.wait_ge(ac.h, ac.n)
                    e.wait_ge(pe.h, pe.n)
                    e.wait_ge(pl.h, pl.n)
                    for nm in B_DBG:
                        t = dbg_map[nm]
                        o = nc.dram_tensor("dbg_" + nm, list(t.shape), t.dtype,
                                           kind="ExternalOutput").ap()
                        if t.shape[0] > 128:
                            for rr in range(0, t.shape[0], 1024):
                                st.tick(e.dma_start(o[rr:rr + 1024, :],
                                                    t[rr:rr + 1024, :]), 16)
                        else:
                            st.tick(e.dma_start(o, t[:, :]), 16)
                    e.wait_ge(st.h, st.n)



def _to_fm(xc):
    return np.ascontiguousarray(
        xc.T.reshape(NFT, 128, xc.shape[0]).transpose(1, 0, 2).reshape(128, -1))


def _progA():
    nc = bass.Bass("TRN2", target_bir_lowering=False)
    cst = Consts(nc)
    x_in = nc.dram_tensor("x_in", [T, 1024], F32, kind="ExternalInput").ap()
    shp = dict(W_q=[1024, 1024], W_k=[1024, 1024], W_v=[1024, 1024], W_o=[1024, 1024],
               b_o=[1024], w_i=[1024, 16], b_i=[16], w_f=[1024, 16], b_f=[16],
               gamma=[1024], beta=[1024])
    w = {k: nc.dram_tensor(k, v, F32, kind="ExternalInput").ap() for k, v in shp.items()}
    x_out = nc.dram_tensor("x_out", [128, NFT * T], F32, kind="ExternalOutput").ap()
    build_phaseA(nc, x_in, w["W_q"], w["W_k"], w["W_v"], w["W_o"], w["b_o"],
                 w["w_i"], w["b_i"], w["w_f"], w["b_f"], w["gamma"], w["beta"],
                 x_out, cst)
    return nc


def _progB():
    nc = bass.Bass("TRN2", target_bir_lowering=False)
    cst = Consts(nc)
    x_in = nc.dram_tensor("x_in", [128, NFT * T], F32, kind="ExternalInput").ap()
    wr = nc.dram_tensor("wr", [64, 1024], F32, kind="ExternalInput").ap()
    eb = nc.dram_tensor("eb", [64], F32, kind="ExternalInput").ap()
    w1 = nc.dram_tensor("w1", [64, 1024, 512], F32, kind="ExternalInput").ap()
    w2 = nc.dram_tensor("w2", [64, 512, 1024], F32, kind="ExternalInput").ap()
    sw1 = nc.dram_tensor("sw1", [2048, 1024], F32, kind="ExternalInput").ap()
    sw2 = nc.dram_tensor("sw2", [1024, 2048], F32, kind="ExternalInput").ap()
    cm = nc.dram_tensor("cm", [8, 1], F32, kind="ExternalInput").ap()
    x_out = nc.dram_tensor("x_out", [128, NFT * T], F32, kind="ExternalOutput").ap()
    build_phaseB(nc, x_in, wr, eb, w1, w2, sw1, sw2, cm, x_out, cst)
    return nc


def _progC():
    nc = bass.Bass("TRN2", target_bir_lowering=False)
    cst = Consts(nc)
    x_in = nc.dram_tensor("x_in", [128, NFT * T], F32, kind="ExternalInput").ap()
    wc = nc.dram_tensor("wc", [10240, 1024], F32, kind="ExternalInput").ap()
    bc = nc.dram_tensor("bc", [10240], F32, kind="ExternalInput").ap()
    wo = nc.dram_tensor("wo", [1024, 8192], F32, kind="ExternalInput").ap()
    bo = nc.dram_tensor("bo", [1024], F32, kind="ExternalInput").ap()
    x_out = nc.dram_tensor("x_out", [128, NFT * T], F32, kind="ExternalOutput").ap()
    build_phaseC(nc, x_in, wc, bc, wo, bo, x_out, cst)
    return nc


def _progD():
    nc = bass.Bass("TRN2", target_bir_lowering=False)
    cst = Consts(nc)
    x_in = nc.dram_tensor("x_in", [128, NFT * T], F32, kind="ExternalInput").ap()
    wg = nc.dram_tensor("wg", [3072, 1024], I32, kind="ExternalInput").ap()
    wu = nc.dram_tensor("wu", [3072, 1024], I32, kind="ExternalInput").ap()
    wd = nc.dram_tensor("wd", [1024, 3072], I32, kind="ExternalInput").ap()
    y = nc.dram_tensor("y", [T, D_MODEL], F32, kind="ExternalOutput").ap()
    build_phaseD(nc, x_in, wg, wu, wd, y, cst, final_tok_out=True)
    return nc


def kernel(**inputs):
    ncores = 8
    x = np.asarray(inputs["L0_attn_x"], np.float32)

    ncA = _progA()
    wA = {k: np.asarray(inputs["L0_attn_" + k], np.float32)
          for k in ["W_q", "W_k", "W_v", "W_o", "b_o", "w_i", "b_i", "w_f", "b_f",
                    "gamma", "beta"]}
    maps = [dict(x_in=np.ascontiguousarray(x[c]), **wA) for c in range(ncores)]
    res = run_bass_kernel_spmd(ncA, maps, core_ids=list(range(ncores)))
    x1 = [res.results[c]["x_out"] for c in range(ncores)]

    ncB = _progB()
    wB = dict(wr=np.asarray(inputs["L0_ffn_router_weight"], np.float32),
              eb=np.asarray(inputs["L0_ffn_expert_bias"], np.float32),
              w1=np.asarray(inputs["L0_ffn_w1"], np.float32),
              w2=np.asarray(inputs["L0_ffn_w2"], np.float32),
              sw1=np.asarray(inputs["L0_ffn_sw1"], np.float32),
              sw2=np.asarray(inputs["L0_ffn_sw2"], np.float32))
    maps = []
    for c in range(ncores):
        cm = np.zeros((8, 1), np.float32)
        cm[:c] = 1.0
        maps.append(dict(x_in=x1[c], cm=cm, **wB))
    res = run_bass_kernel_spmd(ncB, maps, core_ids=list(range(ncores)))
    x2 = [res.results[c]["x_out"] for c in range(ncores)]

    ncC = _progC()
    wC = dict(wc=np.asarray(inputs["L1_attn_Wc"], np.float32),
              bc=np.asarray(inputs["L1_attn_bc"], np.float32),
              wo=np.asarray(inputs["L1_attn_Wo"], np.float32),
              bo=np.asarray(inputs["L1_attn_bo"], np.float32))
    maps = [dict(x_in=x2[c], **wC) for c in range(ncores)]
    res = run_bass_kernel_spmd(ncC, maps, core_ids=list(range(ncores)))
    x3 = [res.results[c]["x_out"] for c in range(ncores)]

    ncD = _progD()
    wD = dict(wg=np.asarray(inputs["L1_ffn_w_gate"], np.int32),
              wu=np.asarray(inputs["L1_ffn_w_up"], np.int32),
              wd=np.asarray(inputs["L1_ffn_w_down"], np.int32))
    maps = [dict(x_in=x3[c], **wD) for c in range(ncores)]
    res = run_bass_kernel_spmd(ncD, maps, core_ids=list(range(ncores)))
    y = np.stack([res.results[c]["y"] for c in range(ncores)])
    return y.astype(np.float32)
